# version 06 (final)
# speedup vs baseline: 1.5321x; 1.0885x over previous
import numpy as np
import ml_dtypes
import concourse.bass as bass
import concourse.mybir as mybir
from concourse.bass_utils import run_bass_kernel_spmd



F32 = mybir.dt.float32
BF16 = mybir.dt.bfloat16
AF = mybir.ActivationFunctionType
ALU = mybir.AluOpType
AX = mybir.AxisListType

ENGINES = ("tensor", "vector", "scalar", "gpsimd", "sync")


class Buf:
    __slots__ = ("name", "last_write", "readers")

    def __init__(self, name):
        self.name = name
        self.last_write = None
        self.readers = []


class Prog:
    def __init__(self, nc):
        self.nc = nc
        self.sems = {}
        self.count = {e: 0 for e in ENGINES}
        self.water = {e: {} for e in ENGINES}
        self.dma_sems = {}
        self.dma_count = {}
        self.queue = {e: [] for e in ENGINES}
        self._ctx = []

    def setup(self, dma_slots=None):
        nc = self.nc
        dma_slots = dma_slots or {}
        for e in ENGINES:
            cm = nc.semaphore(f"S_{e}")
            self.sems[e] = cm.__enter__()
            self._ctx.append(cm)
        for q in ("sync", "gpsimd", "scalar", "vector"):
            ns = dma_slots.get(q, 1)
            slots = []
            for s in range(ns):
                cm = nc.semaphore(f"D_{q}" if s == 0 else f"D_{q}{s}")
                slots.append([cm.__enter__(), 0])
                self._ctx.append(cm)
            self.dma_sems[q] = slots
            self.dma_count[q] = 0

    def _wait(self, consumer, key, val):
        if val <= 0:
            return
        w = self.water[consumer]
        if w.get(key, 0) >= val:
            return
        if key in ENGINES:
            sem = self.sems[key]
        else:
            parts = key.split(":")
            sl = int(parts[2]) if len(parts) > 2 else 0
            sem = self.dma_sems[parts[1]][sl][0]
        self.queue[consumer].append(lambda eng, sem=sem, val=val: eng.wait_ge(sem, val))
        w[key] = val

    def _dep_read(self, consumer, bufs):
        for b in bufs:
            if b is None or b.last_write is None:
                continue
            k, v = b.last_write
            self._wait(consumer, k, v)

    def _dep_write(self, consumer, bufs):
        for b in bufs:
            if b is None:
                continue
            if b.last_write is not None:
                k, v = b.last_write
                self._wait(consumer, k, v)
            for (k, v) in b.readers:
                self._wait(consumer, k, v)

    def _record(self, engine, reads, writes):
        seq = self.count[engine]
        for b in reads:
            b.readers.append((engine, seq))
        for b in writes:
            b.last_write = (engine, seq)
            b.readers = []

    def op(self, engine, fn, reads=(), writes=()):
        reads = [r for r in reads if r is not None]
        writes = [w for w in writes if w is not None]
        self._dep_read(engine, reads)
        self._dep_write(engine, writes)
        sem = self.sems[engine]
        self.queue[engine].append(lambda eng, fn=fn, sem=sem: fn(eng).then_inc(sem, 1))
        self.count[engine] += 1
        self._record(engine, reads, writes)

    def mm_group(self, fn_list, out_buf, reads=()):
        engine = "tensor"
        reads = [r for r in reads if r is not None]
        self._dep_read(engine, reads)
        self._dep_write(engine, [out_buf])
        sem = self.sems[engine]
        n = len(fn_list)
        for i, fn in enumerate(fn_list):
            if i == n - 1:
                self.queue[engine].append(lambda eng, fn=fn, sem=sem: fn(eng).then_inc(sem, 1))
            else:
                self.queue[engine].append(lambda eng, fn=fn: fn(eng))
        self.count[engine] += 1
        self._record(engine, reads, [out_buf])

    def dma(self, queue, out, in_, reads=(), writes=(), **kw):
        reads = [r for r in reads if r is not None]
        writes = [w for w in writes if w is not None]
        slots = self.dma_sems[queue]
        ns = len(slots)
        sl = self.dma_count[queue] % ns
        key = f"dma:{queue}" if ns == 1 else f"dma:{queue}:{sl}"
        self._wait(queue, key, slots[sl][1])
        self._dep_read(queue, reads)
        self._dep_write(queue, writes)
        sem, cnt = slots[sl]
        self.queue[queue].append(
            lambda eng, out=out, in_=in_, sem=sem, kw=kw: eng.dma_start(out=out, in_=in_, **kw).then_inc(sem, 16))
        slots[sl][1] = cnt + 16
        self.dma_count[queue] += 1
        seq = cnt + 16
        for b in reads:
            b.readers.append((key, seq))
        for b in writes:
            b.last_write = (key, seq)
            b.readers = []

    def finish(self, out_bufs):
        for b in out_bufs:
            if b is None or b.last_write is None:
                continue
            k, v = b.last_write
            self._wait("gpsimd", k, v)

    def emit(self, block):
        for e in ENGINES:
            ops = self.queue[e]
            if not ops:
                continue
            def body(eng, ops=ops):
                for c in ops:
                    c(eng)
            getattr(block, e)(body)

    def n_instr(self):
        return sum(len(v) for v in self.queue.values())


def ceil_div(a, b):
    return (a + b - 1) // b


def ts(i, n):
    return bass.ts(i, n)

D = 2048
NTOK = 2048
ST = 512
NST = NTOK // ST
NCH = D // 128
T = 4096
C = 64
NCHK = T // C
SCALE = 128 ** -0.5
GRP = 8


def _ln(p, env):
    (ones128, onesr, epst, srow, qrow, vrow, rrow, nmr, rbcf, nmrbcf, sq,
     P0, P1, P2, PB, bconst, brows, bbc, bsq) = env
    def ln_stats(src, srcbuf):
        fns = [(lambda t, c=c: t.matmul(P0[0:1, :], ones128[:], src[:, c, :],
                                        start=(c == 0), stop=(c == NCH-1))) for c in range(NCH)]
        p.mm_group(fns, PB[0], reads=[srcbuf, bconst])
        p.op("scalar", lambda s: s.activation(srow[:], P0[0:1, :], AF.Copy, scale=1.0/D),
             reads=[PB[0]], writes=[brows])
        for c in range(NCH):
            p.op("scalar", lambda s, c=c: s.activation(sq[:], src[:, c, :], AF.Square),
                 reads=[srcbuf], writes=[bsq])
            p.mm_group([lambda t, c=c: t.matmul(P1[0:1, :], ones128[:], sq[:],
                                                start=(c == 0), stop=(c == NCH-1))],
                       PB[1], reads=[bsq, bconst])
        p.op("scalar", lambda s: s.activation(qrow[:], P1[0:1, :], AF.Copy, scale=1.0/D),
             reads=[PB[1]], writes=[brows])
        p.op("vector", lambda v: v.tensor_tensor(vrow[:], srow[:], srow[:], ALU.mult),
             reads=[brows], writes=[brows])
        p.op("vector", lambda v: v.tensor_tensor(vrow[:], qrow[:], vrow[:], ALU.subtract),
             reads=[brows], writes=[brows])
        p.op("scalar", lambda s: s.activation(vrow[:], vrow[:], AF.Sqrt, bias=epst[:]),
             reads=[brows, bconst], writes=[brows])
        p.op("vector", lambda v: v.reciprocal(rrow[:], vrow[:]), reads=[brows], writes=[brows])
        p.op("vector", lambda v: v.tensor_tensor(nmr[:], srow[:], rrow[:], ALU.mult),
             reads=[brows], writes=[brows])
        p.op("vector", lambda v: v.tensor_scalar(nmr[:], nmr[:], -1.0, None, ALU.mult),
             reads=[brows], writes=[brows])
        p.mm_group([lambda t: t.matmul(P2[:, :], onesr[:], rrow[:], start=True, stop=True)],
                   PB[2], reads=[brows, bconst])
        p.op("scalar", lambda s: s.activation(rbcf[:], P2[:, :], AF.Copy), reads=[PB[2]], writes=[bbc])
        p.mm_group([lambda t: t.matmul(P2[:, :], onesr[:], nmr[:], start=True, stop=True)],
                   PB[2], reads=[brows, bbc, bconst])
        p.op("scalar", lambda s: s.activation(nmrbcf[:], P2[:, :], AF.Copy), reads=[PB[2]], writes=[bbc])
    return ln_stats


def build_l0p1(x_dtype=F32):
    nc = bass.Bass("TRN2", target_bir_lowering=False)
    xT = nc.dram_tensor("xT", [D, NTOK], x_dtype, kind="ExternalInput").ap()
    wqkv = nc.dram_tensor("wqkv", [D, 3 * D], F32, kind="ExternalInput").ap()
    wf1 = nc.dram_tensor("Wf1", [D, 128], F32, kind="ExternalInput").ap()
    wb = nc.dram_tensor("Wb", [D, 16], F32, kind="ExternalInput").ap()
    qkvO = nc.dram_tensor("qkvO", [3 * D, NTOK], BF16, kind="ExternalOutput").ap()
    f1O = nc.dram_tensor("f1O", [128, NTOK], F32, kind="ExternalOutput").ap()
    betaO = nc.dram_tensor("betaO", [16, NTOK], F32, kind="ExternalOutput").ap()
    from contextlib import ExitStack
    es = ExitStack()
    Tn = lambda *a: es.enter_context(nc.sbuf_tensor(*a))
    xst = Tn("xst", [128, NCH, ST], BF16)
    xln = Tn("xln", [128, NCH, ST], BF16)
    sq = Tn("sq", [128, ST], BF16)
    sqf = Tn("sqf", [128, ST], F32)
    wsl = Tn("wsl", [128, 2, 16, 512], BF16)
    outt = Tn("outt", [128, 2, 4, ST], BF16)
    f1t = Tn("f1t", [128, NTOK], F32)
    bt = Tn("bt", [16, NTOK], F32)
    ones128 = Tn("ones128", [128, 1], BF16)
    onesr = Tn("onesr", [1, 128], F32)
    epst = Tn("epst", [1, 1], F32)
    srow = Tn("srow", [1, ST], F32); qrow = Tn("qrow", [1, ST], F32)
    vrow = Tn("vrow", [1, ST], F32); rrow = Tn("rrow", [1, ST], F32)
    nmr = Tn("nmr", [1, ST], F32)
    rbcf = Tn("rbcf", [128, ST], F32); nmrbcf = Tn("nmrbcf", [128, ST], F32)
    P0 = es.enter_context(nc.psum_tensor("P0", [128, 512], F32))
    P1p = es.enter_context(nc.psum_tensor("P1", [128, 512], F32))
    P2p = es.enter_context(nc.psum_tensor("P2", [128, 512], F32))
    P3p = es.enter_context(nc.psum_tensor("P3", [128, 512], F32))
    P4p = es.enter_context(nc.psum_tensor("P4", [128, 512], F32))
    with nc.Block() as block:
        p = Prog(nc); p.setup(dma_slots={"sync": 3, "gpsimd": 3})
        PB = [Buf(f"P{i}") for i in range(5)]
        bxst, bxln, bsq, bsqf = (Buf(n) for n in ("xst", "xln", "sq", "sqf"))
        bwsl = [Buf("wsl0"), Buf("wsl1")]
        bout = [Buf("out0"), Buf("out1")]
        bf1, bbt = Buf("f1t"), Buf("bt")
        bconst, brows, bbc, bO = Buf("const"), Buf("rows"), Buf("bc"), Buf("O")
        p.op("vector", lambda v: v.memset(ones128[:], 1.0), writes=[bconst])
        p.op("vector", lambda v: v.memset(onesr[:], 1.0), writes=[bconst])
        p.op("vector", lambda v: v.memset(epst[:], 1e-5), writes=[bconst])
        ln_stats = _ln(p, (ones128, onesr, epst, srow, qrow, vrow, rrow, nmr,
                           rbcf, nmrbcf, sq, P0, P1p, P2p, PB, bconst, brows, bbc, bsq))
        wqa = wqkv.rearrange("(c p) o -> p c o", p=128)
        for st in range(NST):
            tok0 = st * ST
            q = "gpsimd" if x_dtype == F32 else "sync"
            p.dma(q, out=xst[:], in_=xT.rearrange("(c p) t -> p c t", p=128)[:, :, tok0:tok0+ST],
                  writes=[bxst])
            ln_stats(xst, bxst)
            for c in range(NCH):
                p.op("vector", lambda v, c=c: v.tensor_tensor(sqf[:], xst[:, c, :], rbcf[:], ALU.mult),
                     reads=[bxst, bbc], writes=[bsqf])
                p.op("vector", lambda v, c=c: v.tensor_tensor(xln[:, c, :], sqf[:], nmrbcf[:], ALU.add),
                     reads=[bsqf, bbc], writes=[bxln])
            for og in range(12):
                ws = og % 2
                p.dma("gpsimd", out=wsl[:, ws, :, :], in_=wqa[:, :, og*512:(og+1)*512],
                      writes=[bwsl[ws]])
                for o in range(4):
                    bank = 3 if o % 2 == 0 else 4
                    Pb = P3p if bank == 3 else P4p
                    fns = [(lambda t, k=k, o=o, Pb=Pb, ws=ws: t.matmul(
                        Pb[:, :], wsl[:, ws, k, o*128:(o+1)*128], xln[:, k, :],
                        start=(k == 0), stop=(k == NCH-1))) for k in range(NCH)]
                    p.mm_group(fns, PB[bank], reads=[bwsl[ws], bxln])
                    p.op("scalar", lambda s, o=o, bank=bank, Pb=Pb, ws=ws: s.activation(
                        outt[:, ws, o, :], Pb[:, :], AF.Copy),
                         reads=[PB[bank]], writes=[bout[ws]])
                p.dma("sync", out=qkvO.rearrange("(c p) t -> p c t", p=128)[:, og*4:(og+1)*4, tok0:tok0+ST],
                      in_=outt[:, ws, :, :], reads=[bout[ws]], writes=[bO])
            p.dma("gpsimd", out=wsl[:, 0, :, 0:128], in_=wf1.rearrange("(c p) o -> p c o", p=128),
                  writes=[bwsl[0]])
            fns = [(lambda t, k=k: t.matmul(P3p[:, :], wsl[:, 0, k, 0:128], xln[:, k, :],
                                            start=(k == 0), stop=(k == NCH-1))) for k in range(NCH)]
            p.mm_group(fns, PB[3], reads=[bwsl[0], bxln])
            p.op("scalar", lambda s, st=st: s.activation(f1t[:, st*ST:(st+1)*ST], P3p[:, :], AF.Copy),
                 reads=[PB[3]], writes=[bf1])
            p.dma("gpsimd", out=wsl[:, 1, :, 0:16], in_=wb.rearrange("(c p) o -> p c o", p=128),
                  writes=[bwsl[1]])
            fns = [(lambda t, k=k: t.matmul(P4p[0:16, :], wsl[:, 1, k, 0:16], xln[:, k, :],
                                            start=(k == 0), stop=(k == NCH-1))) for k in range(NCH)]
            p.mm_group(fns, PB[4], reads=[bwsl[1], bxln])
            p.op("scalar", lambda s, st=st: s.activation(bt[:, st*ST:(st+1)*ST], P4p[0:16, :], AF.Copy),
                 reads=[PB[4]], writes=[bbt])
        p.dma("sync", out=f1O[:, :], in_=f1t[:], reads=[bf1], writes=[bO])
        p.dma("sync", out=betaO[:, :], in_=bt[:], reads=[bbt], writes=[bO])
        p.finish([bO])
        p.emit(block)
    es.close()
    return nc, p


LEVELS = (8, 16, 32)
DELTAS = (1, 2, 3, 4, 5, 6, 7)




T = 4096
C = 64
NCHK = T // C
SCALE = 128 ** -0.5
LEVELS = (8, 16, 32)
DELTAS = (1, 2, 3, 4, 5, 6, 7)
NUNITS = 8
DBG = False


def build_l0p2():
    nc = bass.Bass("TRN2", target_bir_lowering=False)
    QR = nc.dram_tensor("QR", [1024, T], BF16, kind="ExternalInput").ap()
    KR = nc.dram_tensor("KR", [1024, T], BF16, kind="ExternalInput").ap()
    VR = nc.dram_tensor("VR", [1024, T], BF16, kind="ExternalInput").ap()
    F1 = nc.dram_tensor("F1", [128, T], F32, kind="ExternalInput").ap()
    BLOG = nc.dram_tensor("BLOG", [8, T], F32, kind="ExternalInput").ap()
    CQ = nc.dram_tensor("CQ", [128, 8, 4], F32, kind="ExternalInput").ap()
    CK = nc.dram_tensor("CK", [128, 8, 4], F32, kind="ExternalInput").ap()
    CV = nc.dram_tensor("CV", [128, 8, 4], F32, kind="ExternalInput").ap()
    WF2 = nc.dram_tensor("WF2", [128, 8, 128], F32, kind="ExternalInput").ap()
    DTB = nc.dram_tensor("DTB", [128, 8], F32, kind="ExternalInput").ap()
    ALOG = nc.dram_tensor("ALOG", [1, 8], F32, kind="ExternalInput").ap()
    ident = nc.dram_tensor("ident", [128, 128], F32, kind="ExternalInput").ap()
    TM = nc.dram_tensor("TMASK", [C, 3 * C], BF16, kind="ExternalInput").ap()
    ON = nc.dram_tensor("ON", [1024, T], F32, kind="ExternalOutput").ap()
    if DBG:
        DBGU = nc.dram_tensor("DBGU", [NCHK, C, 128], F32, kind="ExternalOutput").ap()
        DBGS = nc.dram_tensor("DBGS", [NCHK, 128, 128], F32, kind="ExternalOutput").ap()
        DBGA = nc.dram_tensor("DBGA", [C, 2, C, NCHK], BF16, kind="ExternalOutput").ap()
        DBGL = nc.dram_tensor("DBGL", [128, T], F32, kind="ExternalOutput").ap()
        DBGN = nc.dram_tensor("DBGN", [128, 4], F32, kind="ExternalOutput").ap()
        DBGV = nc.dram_tensor("DBGV", [C, 128], BF16, kind="ExternalOutput").ap()
        DBGB = nc.dram_tensor("DBGB", [C, NCHK], F32, kind="ExternalOutput").ap()
        DBGZ = nc.dram_tensor("DBGZ", [C, 128], BF16, kind="ExternalOutput").ap()
        DBGW = nc.dram_tensor("DBGW", [128, 2 * C], BF16, kind="ExternalOutput").ap()
    AGL = nc.dram_tensor("AGL", [C, 2, C, NCHK], BF16, kind="Internal").ap()
    DRD = nc.dram_tensor("DRD", [1, T + 8], F32, kind="Internal").ap()
    from contextlib import ExitStack
    es = ExitStack()
    Tn = lambda *a: es.enter_context(nc.sbuf_tensor(*a))
    qraw = Tn("qraw", [128, T + 3], BF16)
    kraw = Tn("kraw", [128, T + 3], BF16)
    vraw = Tn("vraw", [128, T + 3], BF16)
    qt = qraw[:, 3:]
    kt = kraw[:, 3:]
    vt = Tn("vt", [128, T], BF16)
    gt = Tn("gt", [128, T], F32)
    Lt = Tn("Lt", [128, T + 1], F32)
    Lc = Tn("Lc", [128, T], F32)
    at = Tn("at", [128, T], BF16)
    atl = Tn("atl", [128, T], BF16)
    khat = Tn("khat", [128, T], BF16)
    rowe = Tn("rowe", [128, 2048], F32)
    rf2 = Tn("rf2", [128, T], BF16)
    cfk = Tn("cfk", [128, T], BF16)
    sqs = Tn("sqs", [128, 512], BF16)
    scr = Tn("scr", [128, 512], F32)
    f1sl = Tn("f1sl", [128, 512], BF16)
    ones128f = Tn("ones128f", [128, 1], F32)
    bcol = Tn("bcol", [C, NCHK], F32)
    stage = Tn("stage", [C, 2, C, NCHK], BF16)
    dct = Tn("dct", [C, C], F32)
    dcsb = Tn("dcsb", [C, C], BF16)
    bt_ = Tn("bt_", [C, C], BF16)
    aqp = Tn("aqp", [C, C], BF16)
    btp = Tn("btp", [C, C], BF16)
    btd = Tn("btd", [C, C], BF16)
    btr = Tn("btr", [C, C], BF16)
    zt = Tn("zt", [C, 128], BF16)
    zev = Tn("zev", [C, 128], BF16)
    ut = Tn("ut", [C, 128], BF16)
    vtm = Tn("vtm", [C, 128], BF16)
    ktm = Tn("ktm", [C, 128], BF16)
    S32 = Tn("S32", [128, 128], F32)
    Sbf = Tn("Sbf", [128, 128], BF16)
    och = Tn("och", [C, 128], F32)
    osq = Tn("osq", [C, 128], F32)
    ostg = Tn("ostg", [128, 4, C], F32)
    ssq = Tn("ssq", [C, 1], F32)
    rstd = Tn("rstd", [C, 1], F32)
    gec = Tn("gec", [128, 1], F32)
    convw = Tn("convw", [128, 12], F32)
    wf2t = Tn("wf2t", [128, 128], BF16)
    dtbc = Tn("dtbc", [128, 1], F32)
    negA = Tn("negA", [128, 1], F32)
    a1 = Tn("a1", [1, 1], F32)
    drow = Tn("drow", [1, T + 8], F32)
    bc128 = Tn("bc128", [128, 512], BF16)
    trimask = Tn("trimask", [C, 3, C], BF16)
    identt = Tn("identt", [128, 128], F32)
    identb = Tn("identb", [128, 128], BF16)
    ones128 = Tn("ones128", [128, 1], BF16)
    onesr = Tn("onesr", [1, 128], F32)
    eps5 = Tn("eps5", [128, 1], F32)
    eps6r = Tn("eps6r", [1, 1], F32)
    P0 = es.enter_context(nc.psum_tensor("P0", [128, 512], F32))
    P1p = es.enter_context(nc.psum_tensor("P1", [128, 512], F32))
    P2p = es.enter_context(nc.psum_tensor("P2", [128, 512], F32))
    P3p = es.enter_context(nc.psum_tensor("P3", [128, 512], F32))
    P4p = es.enter_context(nc.psum_tensor("P4", [128, 512], F32))
    P5p = es.enter_context(nc.psum_tensor("P5", [128, 512], F32))
    P6b = es.enter_context(nc.psum_tensor("P6b", [128, 256], BF16))
    with nc.Block() as block:
        p = Prog(nc); p.setup(dma_slots={"sync": 3, "gpsimd": 3})
        names = ("qraw kraw vraw vt gt Lt Lc at atl khat rowe rf2 cfk sqs scr "
                 "f1sl bcol stage dct dcsb bt_ aqp btp btd btr zt zev ut vtm ktm S32 Sbf "
                 "och osq ostg ssq rstd gec convw wf2t dtbc negA drow bc128 const ON "
                 "AGL DRD P0 P1 P2 P3 P4 P5 P6").split()
        B = {n: Buf(n) for n in names}
        B["qt"] = B["qraw"]; B["kt"] = B["kraw"]
        B["brow"] = B["drow"]; B["f1t"] = B["f1sl"]; B["gam"] = B["gt"]
        bconst = B["const"]
        p.op("vector", lambda v: v.memset(ones128[:], 1.0), writes=[bconst])
        p.op("vector", lambda v: v.memset(ones128f[:], 1.0), writes=[bconst])
        p.op("vector", lambda v: v.memset(onesr[:], 1.0), writes=[bconst])
        p.op("vector", lambda v: v.memset(eps5[:], 1e-5), writes=[bconst])
        p.op("vector", lambda v: v.memset(eps6r[:], 1e-6), writes=[bconst])
        p.op("vector", lambda v: v.memset(qraw[:, 0:3], 0.0), writes=[B["qraw"]])
        p.op("vector", lambda v: v.memset(kraw[:, 0:3], 0.0), writes=[B["kraw"]])
        p.op("vector", lambda v: v.memset(vraw[:, 0:3], 0.0), writes=[B["vraw"]])
        p.op("vector", lambda v: v.memset(Lt[:, 0:1], 0.0), writes=[B["Lt"]])
        p.op("vector", lambda v: v.memset(stage[:], 0.0), writes=[B["stage"]])
        p.op("vector", lambda v: v.memset(cfk[:], 0.0), writes=[B["cfk"]])
        p.op("vector", lambda v: v.memset(rf2[:], 0.0), writes=[B["rf2"]])
        p.op("vector", lambda v: v.memset(rowe[:], 0.0), writes=[B["rowe"]])
        p.op("vector", lambda v: v.memset(drow[:], 0.0), writes=[B["drow"]])
        p.dma("sync", out=identt[:], in_=ident[:, :], writes=[bconst])
        p.op("vector", lambda v: v.tensor_copy(identb[:], identt[:]), reads=[bconst], writes=[bconst])
        p.dma("sync", out=trimask[:], in_=TM.rearrange("a (s b) -> a s b", s=3), writes=[bconst])

        for u in range(NUNITS):
            p.dma("sync", out=qraw[:, 3:], in_=QR[u*128:(u+1)*128, :], writes=[B["qraw"]])
            p.dma("sync", out=kraw[:, 3:], in_=KR[u*128:(u+1)*128, :], writes=[B["kraw"]])
            p.dma("sync", out=vraw[:, 3:], in_=VR[u*128:(u+1)*128, :], writes=[B["vraw"]])
            p.dma("sync", out=convw[:, 0:4], in_=CQ[:, u, :], allow_slow_non_contiguous=True, writes=[B["convw"]])
            p.dma("sync", out=convw[:, 4:8], in_=CK[:, u, :], allow_slow_non_contiguous=True, writes=[B["convw"]])
            p.dma("sync", out=convw[:, 8:12], in_=CV[:, u, :], allow_slow_non_contiguous=True, writes=[B["convw"]])
            p.dma("gpsimd", out=wf2t[:], in_=WF2[:, u, :], writes=[B["wf2t"]])
            p.dma("sync", out=dtbc[:], in_=DTB[:, u:u+1], allow_slow_non_contiguous=True, writes=[B["dtbc"]])
            p.dma("sync", out=drow[0:1, 0:T], in_=BLOG[u:u+1, :], writes=[B["drow"]])
            p.op("scalar", lambda s: s.activation(drow[0:1, 0:T], drow[0:1, 0:T], AF.Sigmoid),
                 reads=[B["drow"]], writes=[B["drow"]])
            p.dma("sync", out=DRD[0:1, 0:T], in_=drow[0:1, 0:T], reads=[B["drow"]],
                  writes=[B["DRD"]])
            p.dma("sync", out=bcol[:, :],
                  in_=DRD[0:1, 0:T].rearrange("one (c w) -> (one w) c", c=NCHK),
                  reads=[B["DRD"]], writes=[B["bcol"]], allow_slow_non_contiguous=True)
            p.dma("sync", out=a1[:], in_=ALOG[0:1, u:u+1], allow_slow_non_contiguous=True, writes=[B["negA"]])
            p.op("scalar", lambda s: s.activation(a1[:], a1[:], AF.Exp), reads=[B["negA"]],
                 writes=[B["negA"]])
            p.mm_group([lambda t: t.matmul(P5p[:, 0:1], onesr[:], a1[:], start=True, stop=True)],
                       B["P5"], reads=[B["negA"], bconst])
            p.op("scalar", lambda s: s.activation(negA[:], P5p[:, 0:1], AF.Copy, scale=-1.0),
                 reads=[B["P5"]], writes=[B["negA"]])
            if DBG and u == 0:
                p.dma("sync", out=DBGN[:, 0:1], in_=negA[:], allow_slow_non_contiguous=True, reads=[B["negA"]], writes=[B["ON"]])
            for (raw, brn, dst, bdn, co) in ((qraw, "qraw", qt, "qt", 0),
                                             (kraw, "kraw", kt, "kt", 4),
                                             (vraw, "vraw", vt, "vt", 8)):
                p.op("vector", lambda v, raw=raw, co=co: v.tensor_scalar(
                    gt[:], raw[:, 0:T], convw[:, co:co+1], None, ALU.mult),
                     reads=[B[brn], B["convw"]], writes=[B["gt"]])
                for i in (1, 2, 3):
                    p.op("vector", lambda v, raw=raw, co=co, i=i: v.scalar_tensor_tensor(
                        gt[:], raw[:, i:i+T], convw[:, co+i:co+i+1], gt[:], ALU.mult, ALU.add),
                         reads=[B[brn], B["convw"], B["gt"]], writes=[B["gt"]])
                p.op("scalar", lambda s, dst=dst: s.activation(dst[:], gt[:], AF.Silu),
                     reads=[B["gt"]], writes=[B[bdn]])
            for (src, bsn) in ((qt, "qt"), (kt, "kt")):
                for sl in range(8):
                    p.op("scalar", lambda s, src=src, sl=sl: s.activation(
                        sqs[:], src[:, sl*512:(sl+1)*512], AF.Square),
                         reads=[B[bsn]], writes=[B["sqs"]])
                    p.mm_group([lambda t: t.matmul(P4p[0:1, :], ones128[:], sqs[:],
                                                   start=True, stop=True)],
                               B["P4"], reads=[B["sqs"], bconst])
                    p.op("scalar", lambda s, sl=sl: s.activation(drow[0:1, sl*512:(sl+1)*512],
                                                                 P4p[0:1, :], AF.Ln, bias=eps6r[:]),
                         reads=[B["P4"]], writes=[B["drow"]])
                p.op("scalar", lambda s: s.activation(drow[0:1, 0:T], drow[0:1, 0:T], AF.Exp,
                                                      scale=-0.5),
                     reads=[B["drow"]], writes=[B["drow"]])
                for sl in range(8):
                    p.mm_group([lambda t, sl=sl: t.matmul(
                        P4p[:, :], onesr[:], drow[0:1, sl*512:(sl+1)*512], start=True, stop=True)],
                        B["P4"], reads=[B["drow"], bconst])
                    p.op("scalar", lambda s: s.activation(bc128[:], P4p[:, :], AF.Copy),
                         reads=[B["P4"]], writes=[B["bc128"]])
                    p.op("vector", lambda v, src=src, sl=sl: v.tensor_tensor(
                        src[:, sl*512:(sl+1)*512], src[:, sl*512:(sl+1)*512], bc128[:], ALU.mult),
                         reads=[B[bsn], B["bc128"]], writes=[B[bsn]])
            for sl in range(8):
                p.dma("gpsimd", out=f1sl[:], in_=F1[:, sl*512:(sl+1)*512], writes=[B["f1t"]])
                p.mm_group([(lambda t, sl=sl: t.matmul(P4p[:, :], wf2t[:], f1sl[:],
                                                       start=True, stop=True))],
                           B["P4"], reads=[B["wf2t"], B["f1t"]])
                p.op("scalar", lambda s: s.activation(scr[:], P4p[:, :], AF.Identity, bias=dtbc[:]),
                     reads=[B["P4"], B["dtbc"]], writes=[B["scr"]])
                gs = lambda sl=sl: gt[:, sl*512:(sl+1)*512]
                p.op("scalar", lambda s, sl=sl: s.activation(gt[:, sl*512:(sl+1)*512], scr[:], AF.Abs),
                     reads=[B["scr"]], writes=[B["gt"]])
                p.op("scalar", lambda s, sl=sl: s.activation(gt[:, sl*512:(sl+1)*512], gt[:, sl*512:(sl+1)*512], AF.Exp, scale=-1.0),
                     reads=[B["gt"]], writes=[B["gt"]])
                p.op("scalar", lambda s, sl=sl: s.activation(gt[:, sl*512:(sl+1)*512], gt[:, sl*512:(sl+1)*512], AF.Ln, bias=ones128f[:]),
                     reads=[B["gt"]], writes=[B["gt"]])
                p.op("vector", lambda v: v.tensor_scalar(scr[:], scr[:], 0.0, None, ALU.max),
                     reads=[B["scr"]], writes=[B["scr"]])
                p.op("vector", lambda v, sl=sl: v.tensor_tensor(gt[:, sl*512:(sl+1)*512], scr[:], gt[:, sl*512:(sl+1)*512], ALU.add),
                     reads=[B["scr"], B["gt"]], writes=[B["gt"]])
                p.op("vector", lambda v, sl=sl: v.tensor_scalar(gt[:, sl*512:(sl+1)*512], gt[:, sl*512:(sl+1)*512],
                                                                negA[:], None, ALU.mult),
                     reads=[B["gt"], B["negA"]], writes=[B["gt"]])
            p.op("vector", lambda v: v.tensor_tensor_scan(
                Lt[:, 1:], ones128[:].broadcast_to([128, T]), gt[:], Lt[:, 0:1], ALU.mult, ALU.add),
                 reads=[B["gt"], B["Lt"], bconst], writes=[B["Lt"]])
            p.op("vector", lambda v: v.tensor_tensor(
                Lc.rearrange("p (c w) -> p c w", c=NCHK),
                Lt[:, 1:].rearrange("p (c w) -> p c w", c=NCHK),
                Lt[:, 0:T:C].rearrange("p (c one) -> p c one", one=1).broadcast_to([128, NCHK, C]),
                ALU.subtract),
                 reads=[B["Lt"]], writes=[B["Lc"]])
            if DBG and u == 0:
                p.dma("sync", out=DBGL[:, :], in_=Lc[:], reads=[B["Lc"]], writes=[B["ON"]])
            p.op("scalar", lambda s: s.activation(gt[:], Lc[:], AF.Exp), reads=[B["Lc"]],
                 writes=[B["gt"]])
            p.op("vector", lambda v: v.tensor_tensor(at[:], kt[:], gt[:], ALU.mult),
                 reads=[B["kt"], B["gt"]], writes=[B["at"]])
            p.op("vector", lambda v: v.tensor_tensor(atl[:], qt[:], gt[:], ALU.mult),
                 reads=[B["qt"], B["gt"]], writes=[B["atl"]])
            p.op("vector", lambda v: v.tensor_tensor(
                gt.rearrange("p (c w) -> p c w", c=NCHK),
                Lc.rearrange("p (c w) -> p c w", c=NCHK)[:, :, C-1:C].broadcast_to([128, NCHK, C]),
                Lc.rearrange("p (c w) -> p c w", c=NCHK), ALU.subtract),
                 reads=[B["Lc"]], writes=[B["gt"]])
            p.op("scalar", lambda s: s.activation(khat[:], gt[:], AF.Exp), reads=[B["gt"]],
                 writes=[B["khat"]])
            p.op("vector", lambda v: v.tensor_tensor(khat[:], khat[:], kt[:], ALU.mult),
                 reads=[B["khat"], B["kt"]], writes=[B["khat"]])
            for lsz in LEVELS:
                nsb = T // (2 * lsz)
                lv = Lc.rearrange("p (s w) -> p s w", s=nsb)
                kv_ = kt.rearrange("p (s w) -> p s w", s=nsb)
                qv_ = qt.rearrange("p (s w) -> p s w", s=nsb)
                re_ = rowe.rearrange("p (s w) -> p s w", s=nsb)
                rfv = rf2.rearrange("p (s two w) -> p s two w", s=nsb, two=2)
                cfv = cfk.rearrange("p (s two w) -> p s two w", s=nsb, two=2)
                p.op("vector", lambda v, lv=lv, re_=re_, lsz=lsz, nsb=nsb: v.tensor_tensor(
                    re_, lv[:, :, lsz:2*lsz],
                    lv[:, :, lsz-1:lsz].broadcast_to([128, nsb, lsz]), ALU.subtract),
                     reads=[B["Lc"]], writes=[B["rowe"]])
                p.op("scalar", lambda s: s.activation(rowe[:, 0:T//2], rowe[:, 0:T//2], AF.Exp),
                     reads=[B["rowe"]], writes=[B["rowe"]])
                p.op("vector", lambda v, rfv=rfv, kv_=kv_, re_=re_, lsz=lsz: v.tensor_tensor(
                    rfv[:, :, 0, :], kv_[:, :, lsz:2*lsz], re_, ALU.mult),
                     reads=[B["kt"], B["rowe"]], writes=[B["rf2"]])
                p.op("vector", lambda v, rfv=rfv, qv_=qv_, re_=re_, lsz=lsz: v.tensor_tensor(
                    rfv[:, :, 1, :], qv_[:, :, lsz:2*lsz], re_, ALU.mult),
                     reads=[B["qt"], B["rowe"]], writes=[B["rf2"]])
                p.op("vector", lambda v, lv=lv, re_=re_, lsz=lsz, nsb=nsb: v.tensor_tensor(
                    re_, lv[:, :, lsz-1:lsz].broadcast_to([128, nsb, lsz]), lv[:, :, 0:lsz],
                    ALU.subtract),
                     reads=[B["Lc"]], writes=[B["rowe"]])
                p.op("scalar", lambda s: s.activation(rowe[:, 0:T//2], rowe[:, 0:T//2], AF.Exp),
                     reads=[B["rowe"]], writes=[B["rowe"]])
                p.op("vector", lambda v, cfv=cfv, kv_=kv_, re_=re_, lsz=lsz: v.tensor_tensor(
                    cfv[:, :, 0, :], kv_[:, :, 0:lsz], re_, ALU.mult),
                     reads=[B["kt"], B["rowe"]], writes=[B["cfk"]])
                nbc = max(1, C // (2 * lsz))
                G = min(NCHK, 512 // (2 * lsz))
                for m in range(nbc):
                    j0 = m * 2 * lsz
                    i0 = j0 + lsz
                    base = 0 if j0 < 32 else 32
                    padj = j0 - base
                    rows = padj + lsz
                    for gi, g0 in enumerate(range(0, NCHK, G)):
                        gn = min(G, NCHK - g0)
                        Pb, bPb = (P0, B["P0"]) if gi % 2 == 0 else (P1p, B["P1"])
                        fns = [(lambda t, ch=ch, m=m, lsz=lsz, base=base, rows=rows,
                                g0=g0, Pb=Pb, nbc=nbc: t.matmul(
                            Pb[base:base+rows, (ch-g0)*2*lsz:(ch-g0+1)*2*lsz],
                            cfk[:, ch*C+base:ch*C+base+rows],
                            rf2[:, (ch*nbc+m)*2*lsz:(ch*nbc+m+1)*2*lsz],
                            start=True, stop=True)) for ch in range(g0, g0+gn)]
                        p.mm_group(fns, bPb, reads=[B["cfk"], B["rf2"]])
                        for side in range(2):
                            p.op("scalar", lambda s, lsz=lsz, base=base, rows=rows, i0=i0,
                                 g0=g0, gn=gn, side=side, Pb=Pb: s.activation(
                                     stage[base:base+rows, side, i0:i0+lsz, g0:g0+gn],
                                     Pb[base:base+rows, 0:gn*2*lsz].rearrange(
                                         "a (c s b) -> a s b c", s=2, b=lsz)[:, side],
                                     AF.Copy),
                                 reads=[bPb], writes=[B["stage"]])
            p.dma("sync", out=AGL.rearrange("j s i c -> j (s i c)"),
                  in_=stage.rearrange("j s i c -> j (s i c)"),
                  reads=[B["stage"]], writes=[B["AGL"]])
            for side in range(2):
                qk, bqk = ("kt", "kt") if side == 0 else ("qt", "qt")
                dlist = list(DELTAS) + ([0] if side == 1 else [])
                for d in dlist:
                    qkt = kt if side == 0 else qt
                    for half in range(2):
                        h0 = half * 2048
                        wd = min(2048, (T - d) - h0)
                        if wd <= 0:
                            continue
                        if d == 0:
                            p.op("vector", lambda v, h0=h0, wd=wd: v.tensor_tensor(
                                rowe[:, 0:wd], qt[:, h0:h0+wd], kt[:, h0:h0+wd], ALU.mult),
                                 reads=[B["qt"], B["kt"]], writes=[B["rowe"]])
                        else:
                            p.op("vector", lambda v, d=d, h0=h0, wd=wd: v.tensor_tensor(
                                rowe[:, 0:wd], Lc[:, h0+d:h0+d+wd], Lc[:, h0:h0+wd], ALU.subtract),
                                 reads=[B["Lc"]], writes=[B["rowe"]])
                            p.op("vector", lambda v, wd=wd: v.tensor_scalar(
                                rowe[:, 0:wd], rowe[:, 0:wd], 0.0, None, ALU.min),
                                 reads=[B["rowe"]], writes=[B["rowe"]])
                            p.op("scalar", lambda s, wd=wd: s.activation(
                                rowe[:, 0:wd], rowe[:, 0:wd], AF.Exp),
                                 reads=[B["rowe"]], writes=[B["rowe"]])
                            p.op("vector", lambda v, d=d, h0=h0, wd=wd, qkt=qkt: v.tensor_tensor(
                                rowe[:, 0:wd], rowe[:, 0:wd], qkt[:, h0+d:h0+d+wd], ALU.mult),
                                 reads=[B["rowe"], B[bqk]], writes=[B["rowe"]])
                            p.op("vector", lambda v, d=d, h0=h0, wd=wd: v.tensor_tensor(
                                rowe[:, 0:wd], rowe[:, 0:wd], kt[:, h0:h0+wd], ALU.mult),
                                 reads=[B["rowe"], B["kt"]], writes=[B["rowe"]])
                        for sl in range((wd + 511) // 512):
                            wl = min(512, wd - sl*512)
                            Pb, bPb = (P4p, B["P4"]) if sl % 2 == 0 else (P0, B["P0"])
                            p.mm_group([(lambda t, sl=sl, wl=wl, Pb=Pb: t.matmul(
                                Pb[0:1, 0:wl], ones128f[:], rowe[:, sl*512:sl*512+wl],
                                start=True, stop=True))], bPb, reads=[B["rowe"], bconst])
                            p.op("scalar", lambda s, sl=sl, wl=wl, h0=h0, d=d, Pb=Pb: s.activation(
                                drow[0:1, h0+d+sl*512:h0+d+sl*512+wl], Pb[0:1, 0:wl], AF.Copy),
                                 reads=[bPb], writes=[B["drow"]])
                    p.dma("sync", out=DRD[:, :], in_=drow[:, :], reads=[B["drow"]],
                          writes=[B["DRD"]])
                    p.dma("sync", out=dct[:, :],
                          in_=DRD[0:1, d:d+T].rearrange("one (c w) -> (one c) w", c=NCHK),
                          reads=[B["DRD"]], writes=[B["dct"]])
                    p.op("tensor", lambda t: t.transpose(P5p[0:C, 0:C], dct[:, :], identt[0:C, 0:C]),
                         reads=[B["dct"], bconst], writes=[B["P5"]])
                    p.op("scalar", lambda s: s.activation(dcsb[:, :], P5p[0:C, 0:C], AF.Copy),
                         reads=[B["P5"]], writes=[B["dcsb"]])
                    p.dma("gpsimd", out=_band_dst(AGL, side, d),
                          in_=dcsb[0:C-d, :] if d > 0 else dcsb[:, :],
                          reads=[B["dcsb"]], writes=[B["AGL"]])
            p.dma("sync", out=stage.rearrange("j s i c -> j (s i c)"),
                  in_=AGL.rearrange("j s i c -> j (s i c)"),
                  reads=[B["AGL"]], writes=[B["stage"]])
            if DBG and u == 0:
                p.dma("sync", out=DBGA.rearrange("j s i c -> j (s i c)"),
                      in_=stage.rearrange("j s i c -> j (s i c)"),
                      reads=[B["stage"]], writes=[B["ON"]])
            p.op("vector", lambda v: v.memset(S32[:], 0.0), writes=[B["S32"]])
            p.op("vector", lambda v: v.memset(Sbf[:], 0.0), writes=[B["Sbf"]])
            for ch in range(NCHK):
                cs = slice(ch*C, (ch+1)*C)
                p.op("vector", lambda v, ch=ch: v.tensor_tensor(
                    bt_[:, :], stage[:, 0, :, ch], trimask[:, 0, :], ALU.mult),
                     reads=[B["stage"], bconst], writes=[B["bt_"]])
                p.op("vector", lambda v, ch=ch: v.tensor_scalar(
                    bt_[:, :], bt_[:, :], bcol[:, ch:ch+1], -1.0, ALU.mult, ALU.mult),
                     reads=[B["bt_"], B["bcol"]], writes=[B["bt_"]])
                p.op("vector", lambda v, ch=ch: v.tensor_tensor(
                    aqp[:, :], stage[:, 1, :, ch], trimask[:, 1, :], ALU.mult),
                     reads=[B["stage"], bconst], writes=[B["aqp"]])
                p.op("tensor", lambda t, cs=cs: t.transpose(P6b[0:C, 0:128], vt[:, cs], identb[:]),
                     reads=[B["vt"], bconst], writes=[B["P6"]])
                p.op("scalar", lambda s: s.activation(vtm[:, :], P6b[0:C, 0:128], AF.Copy),
                     reads=[B["P6"]], writes=[B["vtm"]])
                p.mm_group([(lambda t, cs=cs: t.matmul(P1p[0:C, 0:128], at[:, cs], Sbf[:, :],
                                                       start=True, stop=True))],
                           B["P1"], reads=[B["at"], B["Sbf"]])
                p.op("vector", lambda v: v.tensor_tensor(zt[:, :], vtm[:, :], P1p[0:C, 0:128],
                                                         ALU.subtract),
                     reads=[B["vtm"], B["P1"]], writes=[B["zt"]])
                if DBG and u == 0 and ch == 0:
                    p.dma("gpsimd", out=DBGW[:, 0:C], in_=vt[:, 0:C], reads=[B["vt"]], writes=[B["ON"]])
                    p.dma("gpsimd", out=DBGW[:, C:2*C], in_=vraw[:, 0:C], reads=[B["vraw"]], writes=[B["ON"]])
                    p.dma("gpsimd", out=DBGV[:, :], in_=vtm[:, :], reads=[B["vtm"]], writes=[B["ON"]])
                    p.dma("gpsimd", out=DBGB[:, :], in_=bcol[:, :], reads=[B["bcol"]], writes=[B["ON"]])
                    p.dma("gpsimd", out=DBGZ[:, :], in_=zt[:, :], reads=[B["zt"]], writes=[B["ON"]])
                p.op("vector", lambda v: v.tensor_tensor(btd[:, :], bt_[:, :], trimask[:, 2, :],
                                                         ALU.mult),
                     reads=[B["bt_"], bconst], writes=[B["btd"]])
                p.op("vector", lambda v: v.tensor_copy(btp[:, :], btd[:, :]),
                     reads=[B["btd"]], writes=[B["btp"]])
                for e in range(5):
                    p.mm_group([(lambda t: t.matmul(P2p[0:C, 0:128], btp[:, :], zt[:, :],
                                                    start=True, stop=True))],
                               B["P2"], reads=[B["btp"], B["zt"]])
                    p.op("vector", lambda v: v.tensor_tensor(zt[:, :], zt[:, :],
                                                             P2p[0:C, 0:128], ALU.add),
                         reads=[B["zt"], B["P2"]], writes=[B["zt"]])
                    if e < 4:
                        p.op("tensor", lambda t: t.transpose(P6b[0:C, 0:C], btp[:, :],
                                                             identb[0:C, 0:C]),
                             reads=[B["btp"]], writes=[B["P6"]])
                        p.op("scalar", lambda s: s.activation(btr[:, :], P6b[0:C, 0:C], AF.Copy),
                             reads=[B["P6"]], writes=[B["btr"]])
                        p.mm_group([(lambda t: t.matmul(P3p[0:C, 0:C], btr[:, :], btp[:, :],
                                                        start=True, stop=True))],
                                   B["P3"], reads=[B["btr"], B["btp"]])
                        p.op("scalar", lambda s: s.activation(btp[:, :], P3p[0:C, 0:C], AF.Copy),
                             reads=[B["P3"]], writes=[B["btp"]])
                p.mm_group([(lambda t: t.matmul(P2p[0:32, 0:128], bt_[0:32, 32:64], zt[0:32, :],
                                                start=True, stop=True))],
                           B["P2"], reads=[B["bt_"], B["zt"]])
                p.op("vector", lambda v: v.tensor_tensor(zt[32:64, :], vtm[32:64, :],
                                                         P1p[32:64, 0:128], ALU.subtract),
                     reads=[B["vtm"], B["P1"], B["zt"]], writes=[B["zt"]])
                p.op("vector", lambda v: v.tensor_tensor(zt[32:64, :], zt[32:64, :],
                                                         P2p[0:32, 0:128], ALU.add),
                     reads=[B["zt"], B["P2"]], writes=[B["zt"]])
                p.op("vector", lambda v: v.tensor_copy(btp[:, :], btd[:, :]),
                     reads=[B["btd"]], writes=[B["btp"]])
                for e in range(5):
                    p.mm_group([(lambda t: t.matmul(P2p[0:32, 0:128], btp[32:64, 32:64],
                                                    zt[32:64, :], start=True, stop=True))],
                               B["P2"], reads=[B["btp"], B["zt"]])
                    p.op("vector", lambda v: v.tensor_tensor(zt[32:64, :], zt[32:64, :],
                                                             P2p[0:32, 0:128], ALU.add),
                         reads=[B["zt"], B["P2"]], writes=[B["zt"]])
                    if e < 4:
                        p.op("tensor", lambda t: t.transpose(P6b[0:C, 0:C], btp[:, :],
                                                             identb[0:C, 0:C]),
                             reads=[B["btp"]], writes=[B["P6"]])
                        p.op("scalar", lambda s: s.activation(btr[:, :], P6b[0:C, 0:C], AF.Copy),
                             reads=[B["P6"]], writes=[B["btr"]])
                        p.mm_group([(lambda t: t.matmul(P3p[0:C, 0:C], btr[:, :], btp[:, :],
                                                        start=True, stop=True))],
                                   B["P3"], reads=[B["btr"], B["btp"]])
                        p.op("scalar", lambda s: s.activation(btp[:, :], P3p[0:C, 0:C], AF.Copy),
                             reads=[B["P3"]], writes=[B["btp"]])
                p.op("vector", lambda v, ch=ch: v.tensor_scalar(ut[:, :], zt[:, :],
                                                                bcol[:, ch:ch+1], None, ALU.mult),
                     reads=[B["zt"], B["bcol"]], writes=[B["ut"]])
                if DBG and u == 0:
                    p.dma("gpsimd", out=DBGU[ch, :, :], in_=ut[:, :], reads=[B["ut"]],
                          writes=[B["ON"]])
                p.mm_group([
                    (lambda t, cs=cs: t.matmul(P1p[0:C, 0:128], atl[:, cs], Sbf[:, :],
                                               start=True, stop=False)),
                    (lambda t: t.matmul(P1p[0:C, 0:128], aqp[:, :], ut[:, :],
                                        start=False, stop=True)),
                ], B["P1"], reads=[B["atl"], B["Sbf"], B["aqp"], B["ut"]])
                p.op("scalar", lambda s: s.activation(och[:, :], P1p[0:C, 0:128], AF.Copy,
                                                      scale=SCALE),
                     reads=[B["P1"]], writes=[B["och"]])
                p.op("scalar", lambda s: s.activation(osq[:, :], och[:, :], AF.Square,
                                                      accum_out=ssq[:]),
                     reads=[B["och"]], writes=[B["osq"], B["ssq"]])
                p.op("scalar", lambda s: s.activation(rstd[:, :], ssq[:, :], AF.Sqrt,
                                                      scale=1.0/128.0, bias=eps5[0:C, :]),
                     reads=[B["ssq"], bconst], writes=[B["rstd"]])
                p.op("vector", lambda v: v.reciprocal(rstd[:, :], rstd[:, :]),
                     reads=[B["rstd"]], writes=[B["rstd"]])
                p.op("vector", lambda v: v.tensor_scalar(
                    och[:, :], och[:, :], rstd[:, :], None, ALU.mult),
                     reads=[B["och"], B["rstd"]], writes=[B["och"]])
                p.op("tensor", lambda t: t.transpose(P4p[:, 0:C], och[:, :], identt[0:C, 0:C]),
                     reads=[B["och"], bconst], writes=[B["P4"]])
                p.op("scalar", lambda s, ch=ch: s.activation(ostg[:, ch % 4, :], P4p[:, 0:C],
                                                             AF.Copy),
                     reads=[B["P4"]], writes=[B["ostg"]])
                if ch % 4 == 3:
                    p.dma("gpsimd", out=ON[u*128:(u+1)*128, (ch-3)*C:(ch+1)*C].rearrange(
                              "p (f w) -> p f w", f=4),
                          in_=ostg[:], reads=[B["ostg"]], writes=[B["ON"]])
                p.op("tensor", lambda t, cs=cs: t.transpose(P6b[0:C, 0:128], khat[:, cs], identb[:]),
                     reads=[B["khat"], bconst], writes=[B["P6"]])
                p.op("scalar", lambda s: s.activation(ktm[:, :], P6b[0:C, 0:128], AF.Copy),
                     reads=[B["P6"]], writes=[B["ktm"]])
                p.mm_group([(lambda t: t.matmul(P3p[:, 0:128], ktm[:, :], ut[:, :],
                                                start=True, stop=True))],
                           B["P3"], reads=[B["ktm"], B["ut"]])
                p.op("scalar", lambda s, ch=ch: s.activation(gec[:, :], Lc[:, ch*C+C-1:ch*C+C],
                                                             AF.Exp),
                     reads=[B["Lc"]], writes=[B["gec"]])
                p.op("vector", lambda v: v.scalar_tensor_tensor(
                    S32[:, :], S32[:, :], gec[:, :], P3p[:, 0:128], ALU.mult, ALU.add),
                     reads=[B["S32"], B["gec"], B["P3"]], writes=[B["S32"]])
                p.op("vector", lambda v: v.tensor_copy(Sbf[:, :], S32[:, :]),
                     reads=[B["S32"]], writes=[B["Sbf"]])
                if DBG and u == 0:
                    p.dma("sync", out=DBGS[ch, :, :], in_=S32[:, :], reads=[B["S32"]],
                          writes=[B["ON"]])
        p.finish([B["ON"]])
        p.emit(block)
    es.close()
    return nc, p


def _band_dst(AGL, side, d):
    import concourse.bass as bass
    flat = AGL.rearrange("j s i c -> (j s i c)")
    off = side * C * NCHK + d * NCHK
    return bass.AP(flat.tensor, off, [[2 * C * NCHK + NCHK, C - d], [1, NCHK]])

D = 2048
NTOK = 2048
ST = 512
NST = NTOK // ST
NCH = D // 128


def build_l0p3(x_dtype=F32, out_dtype=F32):
    nc = bass.Bass("TRN2", target_bir_lowering=False)
    xT = nc.dram_tensor("xT", [D, NTOK], x_dtype, kind="ExternalInput").ap()
    onFd = nc.dram_tensor("onF", [D, NTOK], F32, kind="ExternalInput").ap()
    Wg = nc.dram_tensor("Wg", [D, D], F32, kind="ExternalInput").ap()
    Wo = nc.dram_tensor("Wo", [D, D], F32, kind="ExternalInput").ap()
    bg = nc.dram_tensor("bg", [128, NCH], F32, kind="ExternalInput").ap()
    onw = nc.dram_tensor("onw", [128, 1], F32, kind="ExternalInput").ap()
    xO = nc.dram_tensor("xO", [D, NTOK], out_dtype, kind="ExternalOutput").ap()
    from contextlib import ExitStack
    es = ExitStack()
    Tn = lambda *a: es.enter_context(nc.sbuf_tensor(*a))
    xst = Tn("xst", [128, NCH, ST], BF16)
    xln = Tn("xln", [128, NCH, ST], BF16)
    ont = Tn("ont", [128, NCH, ST], F32)
    xf32 = Tn("xf32", [128, 512], F32)
    gbuf = Tn("gbuf", [128, NCH, ST], BF16)
    wsl = Tn("wsl", [128, 2, 16, 512], BF16)
    ffno = Tn("ffno", [128, NCH, ST], BF16)
    sq = Tn("sq", [128, ST], BF16)
    sqf = Tn("sqf", [128, ST], F32)
    onwt = Tn("onwt", [128, 1], F32)
    bgt = Tn("bgt", [128, NCH], F32)
    ones128 = Tn("ones128", [128, 1], BF16)
    onesr = Tn("onesr", [1, 128], F32)
    epst = Tn("epst", [1, 1], F32)
    srow = Tn("srow", [1, ST], F32); qrow = Tn("qrow", [1, ST], F32)
    vrow = Tn("vrow", [1, ST], F32); rrow = Tn("rrow", [1, ST], F32)
    nmr = Tn("nmr", [1, ST], F32)
    rbcf = Tn("rbcf", [128, ST], F32); nmrbcf = Tn("nmrbcf", [128, ST], F32)
    P0 = es.enter_context(nc.psum_tensor("P0", [128, 512], F32))
    P1p = es.enter_context(nc.psum_tensor("P1", [128, 512], F32))
    P2p = es.enter_context(nc.psum_tensor("P2", [128, 512], F32))
    P3p = es.enter_context(nc.psum_tensor("P3", [128, 512], F32))
    P4p = es.enter_context(nc.psum_tensor("P4", [128, 512], F32))
    with nc.Block() as block:
        p = Prog(nc); p.setup(dma_slots={"sync": 3, "gpsimd": 3})
        PB = [Buf(f"P{i}") for i in range(5)]
        bxst, bxln, bont, bgbuf, bffno, bxf32 = (Buf(n) for n in
            ("xst", "xln", "ont", "gbuf", "ffno", "xf32"))
        bwsl = [Buf("wsl0"), Buf("wsl1")]
        bsq, bsqf, bconst, brows, bbc, bO, bonw, bbg = (Buf(n) for n in
            ("sq", "sqf", "const", "rows", "bc", "O", "onw", "bg"))
        p.op("vector", lambda v: v.memset(ones128[:], 1.0), writes=[bconst])
        p.op("vector", lambda v: v.memset(onesr[:], 1.0), writes=[bconst])
        p.op("vector", lambda v: v.memset(epst[:], 1e-5), writes=[bconst])
        p.dma("sync", out=onwt[:], in_=onw[:, :], writes=[bonw])
        p.dma("sync", out=bgt[:], in_=bg[:, :], writes=[bbg])

        def ln_stats(src, srcbuf):
            fns = [(lambda t, c=c: t.matmul(P0[0:1, :], ones128[:], src[:, c, :],
                                            start=(c == 0), stop=(c == NCH-1))) for c in range(NCH)]
            p.mm_group(fns, PB[0], reads=[srcbuf, bconst])
            p.op("scalar", lambda s: s.activation(srow[:], P0[0:1, :], AF.Copy, scale=1.0/D),
                 reads=[PB[0]], writes=[brows])
            for c in range(NCH):
                p.op("scalar", lambda s, c=c: s.activation(sq[:], src[:, c, :], AF.Square),
                     reads=[srcbuf], writes=[bsq])
                p.mm_group([lambda t, c=c: t.matmul(P1p[0:1, :], ones128[:], sq[:],
                                                    start=(c == 0), stop=(c == NCH-1))],
                           PB[1], reads=[bsq, bconst])
            p.op("scalar", lambda s: s.activation(qrow[:], P1p[0:1, :], AF.Copy, scale=1.0/D),
                 reads=[PB[1]], writes=[brows])
            p.op("vector", lambda v: v.tensor_tensor(vrow[:], srow[:], srow[:], ALU.mult),
                 reads=[brows], writes=[brows])
            p.op("vector", lambda v: v.tensor_tensor(vrow[:], qrow[:], vrow[:], ALU.subtract),
                 reads=[brows], writes=[brows])
            p.op("scalar", lambda s: s.activation(vrow[:], vrow[:], AF.Sqrt, bias=epst[:]),
                 reads=[brows, bconst], writes=[brows])
            p.op("vector", lambda v: v.reciprocal(rrow[:], vrow[:]), reads=[brows], writes=[brows])
            p.op("vector", lambda v: v.tensor_tensor(nmr[:], srow[:], rrow[:], ALU.mult),
                 reads=[brows], writes=[brows])
            p.op("vector", lambda v: v.tensor_scalar(nmr[:], nmr[:], -1.0, None, ALU.mult),
                 reads=[brows], writes=[brows])
            p.mm_group([lambda t: t.matmul(P2p[:, :], onesr[:], rrow[:], start=True, stop=True)],
                       PB[2], reads=[brows, bconst])
            p.op("scalar", lambda s: s.activation(rbcf[:], P2p[:, :], AF.Copy), reads=[PB[2]], writes=[bbc])
            p.mm_group([lambda t: t.matmul(P2p[:, :], onesr[:], nmr[:], start=True, stop=True)],
                       PB[2], reads=[brows, bbc, bconst])
            p.op("scalar", lambda s: s.activation(nmrbcf[:], P2p[:, :], AF.Copy), reads=[PB[2]], writes=[bbc])

        wga = Wg.rearrange("(c p) o -> p c o", p=128)
        woa = Wo.rearrange("(c p) o -> p c o", p=128)
        for st in range(NST):
            tok0 = st * ST
            q = "gpsimd" if x_dtype == F32 else "sync"
            p.dma(q, out=xst[:], in_=xT.rearrange("(c p) t -> p c t", p=128)[:, :, tok0:tok0+ST],
                  writes=[bxst])
            p.dma("sync", out=ont[:], in_=onFd.rearrange("(c p) t -> p c t", p=128)[:, :, tok0:tok0+ST],
                  writes=[bont])
            ln_stats(xst, bxst)
            for c in range(NCH):
                p.op("vector", lambda v, c=c: v.tensor_tensor(sqf[:], xst[:, c, :], rbcf[:], ALU.mult),
                     reads=[bxst, bbc], writes=[bsqf])
                p.op("vector", lambda v, c=c: v.tensor_tensor(xln[:, c, :], sqf[:], nmrbcf[:], ALU.add),
                     reads=[bsqf, bbc], writes=[bxln])
            for og in range(4):
                ws = og % 2
                p.dma("gpsimd", out=wsl[:, ws, :, :], in_=wga[:, :, og*512:(og+1)*512],
                      writes=[bwsl[ws]])
                for o in range(4):
                    oc = og * 4 + o
                    bank = 3 if o % 2 == 0 else 4
                    Pb = P3p if bank == 3 else P4p
                    fns = [(lambda t, k=k, o=o, Pb=Pb, ws=ws: t.matmul(
                        Pb[:, :], wsl[:, ws, k, o*128:(o+1)*128], xln[:, k, :],
                        start=(k == 0), stop=(k == NCH-1))) for k in range(NCH)]
                    p.mm_group(fns, PB[bank], reads=[bwsl[ws], bxln])
                    p.op("scalar", lambda s, oc=oc, bank=bank, Pb=Pb: s.activation(
                        gbuf[:, oc, :], Pb[:, :], AF.Silu, bias=bgt[:, oc:oc+1]),
                         reads=[PB[bank], bbg], writes=[bgbuf])
            for c in range(NCH):
                p.op("vector", lambda v, c=c: v.tensor_tensor(
                    gbuf[:, c, :], gbuf[:, c, :], ont[:, c, :], ALU.mult),
                     reads=[bgbuf, bont], writes=[bgbuf])
            for og in range(4):
                ws = og % 2
                p.dma("gpsimd", out=wsl[:, ws, :, :], in_=woa[:, :, og*512:(og+1)*512],
                      writes=[bwsl[ws]])
                for c in range(NCH):
                    p.op("vector", lambda v, c=c, ws=ws: v.tensor_scalar(
                        wsl[:, ws, c, :], wsl[:, ws, c, :], onwt[:], None, ALU.mult),
                         reads=[bwsl[ws], bonw], writes=[bwsl[ws]])
                for o in range(4):
                    oc = og * 4 + o
                    bank = 3 if o % 2 == 0 else 4
                    Pb = P3p if bank == 3 else P4p
                    fns = [(lambda t, k=k, o=o, Pb=Pb, ws=ws: t.matmul(
                        Pb[:, :], wsl[:, ws, k, o*128:(o+1)*128], gbuf[:, k, :],
                        start=(k == 0), stop=(k == NCH-1))) for k in range(NCH)]
                    p.mm_group(fns, PB[bank], reads=[bwsl[ws], bgbuf])
                    p.op("scalar", lambda s, oc=oc, bank=bank, Pb=Pb: s.activation(
                        ffno[:, oc, :], Pb[:, :], AF.Copy),
                         reads=[PB[bank]], writes=[bffno])
            ln_stats(ffno, bffno)
            for c in range(NCH):
                p.dma("sync", out=xf32[:],
                      in_=xT.rearrange("(c p) t -> p c t", p=128)[:, c, tok0:tok0+ST],
                      writes=[bxf32])
                p.op("vector", lambda v, c=c: v.tensor_tensor(sqf[:], ffno[:, c, :], rbcf[:], ALU.mult),
                     reads=[bffno, bbc], writes=[bsqf])
                p.op("vector", lambda v, c=c: v.tensor_tensor(sqf[:], sqf[:], nmrbcf[:], ALU.add),
                     reads=[bsqf, bbc], writes=[bsqf])
                p.op("vector", lambda v, c=c: v.tensor_tensor(xf32[:], xf32[:], sqf[:], ALU.add),
                     reads=[bxf32, bsqf], writes=[bxf32])
                p.dma("gpsimd", out=xO.rearrange("(c p) t -> p c t", p=128)[:, c, tok0:tok0+ST],
                      in_=xf32[:], reads=[bxf32], writes=[bO])
        p.finish([bO])
        p.emit(block)
    es.close()
    return nc, p

HID0 = [1024, 2048, 3072, 4096, 1024, 2048, 3072, 4096]
COST_LAMBDA0 = 1e-07
D = 2048
L = 1024
NTOK = 2048
ST = 512
NST = NTOK // ST
NCH = D // 128
LCH0 = L // 128


def build_l0ffn(x_dtype=F32, out_dtype=F32):
    nc = bass.Bass("TRN2", target_bir_lowering=False)
    xT = nc.dram_tensor("xT", [D, NTOK], x_dtype, kind="ExternalInput").ap()
    W = {}
    def din(name, shape):
        W[name] = nc.dram_tensor(name, shape, F32, kind="ExternalInput").ap()
    din("down_w", [D, L]); din("down_b", [128, L // 128])
    din("up_w", [L, D]); din("up_b", [128, D // 128])
    din("router_w", [D, 8]); din("router_b", [8])
    din("shared_w1", [L, L]); din("shared_b1", [128, L // 128])
    din("shared_w2", [L, L]); din("shared_b2", [128, L // 128])
    din("core_w", [D, D]); din("core_b", [128, D // 128])
    for e in range(8):
        din(f"e{e}_w1", [L, HID0[e]]); din(f"e{e}_b1", [128, HID0[e] // 128])
        din(f"e{e}_w2", [HID0[e], L]); din(f"e{e}_b2", [L])
    din("ident", [128, 128])
    din("clc", [1, 8])
    xO = nc.dram_tensor("xO", [D, NTOK], out_dtype, kind="ExternalOutput").ap()

    from contextlib import ExitStack
    es = ExitStack()
    xst = es.enter_context(nc.sbuf_tensor("xst", [128, NCH, ST], BF16))
    xln = es.enter_context(nc.sbuf_tensor("xln", [128, NCH, ST], BF16))
    gxln = es.enter_context(nc.sbuf_tensor("gxln", [128, NCH, ST], BF16))
    gh = es.enter_context(nc.sbuf_tensor("gh", [128, LCH0, ST], BF16))
    gtile = es.enter_context(nc.sbuf_tensor("gt", [128, 8, ST], BF16))
    out2 = es.enter_context(nc.sbuf_tensor("out2", [128, LCH0, ST], F32))
    ffno = es.enter_context(nc.sbuf_tensor("ffno", [128, NCH, ST], BF16))
    sq = es.enter_context(nc.sbuf_tensor("sq", [128, ST], BF16))
    sqf = es.enter_context(nc.sbuf_tensor("sqf", [128, ST], F32))
    rwt = es.enter_context(nc.sbuf_tensor("rwt", [128, NCH, 8], BF16))
    upt = es.enter_context(nc.sbuf_tensor("upt", [128, LCH0, 512], BF16))
    sh1t = es.enter_context(nc.sbuf_tensor("sh1t", [128, LCH0, L], BF16))
    sh2t = es.enter_context(nc.sbuf_tensor("sh2t", [128, LCH0, L], BF16))
    wbuf = es.enter_context(nc.sbuf_tensor("wbuf", [128, 2, 8, 512], BF16))
    cwb = es.enter_context(nc.sbuf_tensor("cwb", [128, NCH, 512], BF16))
    xnew = xln
    ones128 = es.enter_context(nc.sbuf_tensor("ones128", [128, 1], BF16))
    onesr = es.enter_context(nc.sbuf_tensor("onesr", [1, 128], F32))
    onesrb = es.enter_context(nc.sbuf_tensor("onesrb", [1, 128], BF16))
    identt = es.enter_context(nc.sbuf_tensor("identt", [128, 128], F32))
    b1t = es.enter_context(nc.sbuf_tensor("b1t", [128, 32], F32))
    biasc = es.enter_context(nc.sbuf_tensor("biasc", [128, NCH], F32))
    cbt = es.enter_context(nc.sbuf_tensor("cbt", [128, NCH], F32))
    dbt = es.enter_context(nc.sbuf_tensor("dbt", [128, LCH0], F32))
    sb1c = es.enter_context(nc.sbuf_tensor("sb1c", [128, LCH0], F32))
    shb2c = es.enter_context(nc.sbuf_tensor("shb2c", [128, LCH0], F32))
    rbrow = es.enter_context(nc.sbuf_tensor("rbrow", [1, 8], BF16))
    clcrow = es.enter_context(nc.sbuf_tensor("clcrow", [1, 8], F32))
    rbrowf = es.enter_context(nc.sbuf_tensor("rbrowf", [1, 8], F32))
    srow = es.enter_context(nc.sbuf_tensor("srow", [1, ST], F32))
    qrow = es.enter_context(nc.sbuf_tensor("qrow", [1, ST], F32))
    vrow = es.enter_context(nc.sbuf_tensor("vrow", [1, ST], F32))
    rrow = es.enter_context(nc.sbuf_tensor("rrow", [1, ST], F32))
    nmr = es.enter_context(nc.sbuf_tensor("nmr", [1, ST], F32))
    rbcf = es.enter_context(nc.sbuf_tensor("rbcf", [128, ST], F32))
    nmrbcf = es.enter_context(nc.sbuf_tensor("nmrbcf", [128, ST], F32))
    m1 = es.enter_context(nc.sbuf_tensor("m1", [128, 1], F32))
    pexp = es.enter_context(nc.sbuf_tensor("pexp", [128, 8], F32))
    ssum = es.enter_context(nc.sbuf_tensor("ssum", [128, 1], F32))
    rsum = es.enter_context(nc.sbuf_tensor("rsum", [128, 1], F32))
    prob = es.enter_context(nc.sbuf_tensor("prob", [128, 8], F32))
    v1t = es.enter_context(nc.sbuf_tensor("v1", [128, 1], F32))
    v2t = es.enter_context(nc.sbuf_tensor("v2", [128, 1], F32))
    mask1 = es.enter_context(nc.sbuf_tensor("mask1", [128, 8], F32))
    mask2 = es.enter_context(nc.sbuf_tensor("mask2", [128, 8], F32))
    ptmp = es.enter_context(nc.sbuf_tensor("ptmp", [128, 8], F32))
    p2t = es.enter_context(nc.sbuf_tensor("p2", [128, 8], F32))
    evt = es.enter_context(nc.sbuf_tensor("ev", [128, 2], F32))
    dent = es.enter_context(nc.sbuf_tensor("den", [128, 1], F32))
    rdent = es.enter_context(nc.sbuf_tensor("rden", [128, 1], F32))
    g12 = es.enter_context(nc.sbuf_tensor("g12", [128, 2], F32))
    lam = es.enter_context(nc.sbuf_tensor("lam", [128, 8], F32))
    lamT = es.enter_context(nc.sbuf_tensor("lamT", [8, NTOK // 128, 128], BF16))
    lamb9 = es.enter_context(nc.sbuf_tensor("lamb9", [9, ST], BF16))
    lrow = es.enter_context(nc.sbuf_tensor("lrow", [1, ST], BF16))
    lbc = es.enter_context(nc.sbuf_tensor("lbc", [128, ST], BF16))
    dmat = es.enter_context(nc.sbuf_tensor("dmat", [9, L], BF16))
    drowb = es.enter_context(nc.sbuf_tensor("drowb", [1, L], BF16))
    b2b = es.enter_context(nc.sbuf_tensor("b2b", [8, L], BF16))
    ones8 = es.enter_context(nc.sbuf_tensor("ones8", [8, 1], BF16))
    negones8 = es.enter_context(nc.sbuf_tensor("negones8", [8, 1], BF16))
    gb1 = es.enter_context(nc.sbuf_tensor("gb1", [128, 32], F32))
    gb1b = es.enter_context(nc.sbuf_tensor("gb1b", [128, 32], BF16))
    epst = es.enter_context(nc.sbuf_tensor("epst", [1, 1], F32))
    xf32 = es.enter_context(nc.sbuf_tensor("xf32", [128, 512], F32))
    xnf = es.enter_context(nc.sbuf_tensor("xnf", [128, 512], F32))
    P0 = es.enter_context(nc.psum_tensor("P0", [128, 512], F32))
    P1 = es.enter_context(nc.psum_tensor("P1", [128, 512], F32))
    P2 = es.enter_context(nc.psum_tensor("P2", [128, 512], F32))
    P3 = es.enter_context(nc.psum_tensor("P3", [128, 512], F32))
    P4 = es.enter_context(nc.psum_tensor("P4", [128, 512], F32))
    P5 = es.enter_context(nc.psum_tensor("P5", [128, 512], F32))
    P6 = es.enter_context(nc.psum_tensor("P6", [128, 512], F32))
    with nc.Block() as block:

        p = Prog(nc)
        p.setup(dma_slots={"sync": 3, "gpsimd": 3})
        wtog = [0]
        PS = [P0, P1, P2, P3, P4, P5, P6]
        PB = [Buf(f"P{i}") for i in range(7)]

        bxst, bxln, bgxln, bgh, bgt = Buf("xst"), Buf("xln"), Buf("gxln"), Buf("gh"), Buf("gt")
        bout2, bffno = Buf("out2"), Buf("ffno")
        out2b = gh
        bout2b = bgh
        bsq, bsqf = Buf("sq"), Buf("sqf")
        brwt, bupt, bsh1, bsh2, bcwb = (Buf(n) for n in
            ("rwt", "upt", "sh1t", "sh2t", "cwb"))
        bwbuf = [Buf("wbuf0"), Buf("wbuf1")]
        bconst = Buf("const")
        bb1, bbias, bdb, bsb1, bshb2 = Buf("b1t"), Buf("biasc"), Buf("dbt"), Buf("sb1c"), Buf("shb2c")
        brows, bbc, brt = Buf("statrows"), Buf("bcast"), Buf("routerscratch")
        blam, blamT, blam9, blbc, blrow = Buf("lam"), Buf("lamT"), Buf("lamb9"), Buf("lbc"), Buf("lrow")
        bdmat, bdrow, bb2b, bgb1 = Buf("dmat"), Buf("drowb"), Buf("b2b"), Buf("gb1")
        bxO = Buf("xO")
        bxf32, bxnf = Buf("xf32"), Buf("xnf")

        p.op("vector", lambda v: v.memset(ones128[:], 1.0), writes=[bconst])
        p.op("vector", lambda v: v.memset(onesr[:], 1.0), writes=[bconst])
        p.op("vector", lambda v: v.memset(onesrb[:], 1.0), writes=[bconst])
        p.op("vector", lambda v: v.memset(ones8[:], 1.0), writes=[bconst])
        p.op("vector", lambda v: v.memset(negones8[:], -1.0), writes=[bconst])
        p.op("vector", lambda v: v.memset(epst[:], 1e-5), writes=[bconst])
        p.dma("sync", out=identt[:], in_=W["ident"][:, :], writes=[bconst])
        p.dma("sync", out=clcrow[:], in_=W["clc"][:, :], writes=[bconst])
        p.dma("sync", out=rbrowf[:], in_=W["router_b"][None, :], writes=[bconst])
        p.op("vector", lambda v: v.tensor_tensor(rbrow[:], rbrowf[:], clcrow[:], ALU.subtract),
             reads=[bconst], writes=[bconst])
        p.dma("sync", out=dbt[:], in_=W["down_b"][:, :], writes=[bdb])
        p.dma("sync", out=sb1c[:], in_=W["shared_b1"][:, :], writes=[bsb1])
        p.dma("sync", out=shb2c[:], in_=W["shared_b2"][:, :], writes=[bshb2])
        p.dma("sync", out=biasc[:], in_=W["up_b"][:, :], writes=[bbias])
        p.dma("sync", out=cbt[:], in_=W["core_b"][:, :], writes=[bbias])
        p.op("vector", lambda v: v.tensor_tensor(biasc[:], biasc[:], cbt[:], ALU.add),
             reads=[bbias], writes=[bbias])
        for e in range(8):
            p.dma("gpsimd", out=b2b[e:e+1, :], in_=W[f"e{e}_b2"][None, :], writes=[bb2b])
        p.dma("gpsimd", out=rwt[:], in_=W["router_w"].rearrange("(c p) o -> p c o", p=128), writes=[brwt])
        p.dma("gpsimd", out=sh1t[:], in_=W["shared_w1"].rearrange("(c p) o -> p c o", p=128), writes=[bsh1])
        p.dma("gpsimd", out=sh2t[:], in_=W["shared_w2"].rearrange("(c p) o -> p c o", p=128), writes=[bsh2])

        for e in range(8):
            hch = HID0[e] // 128
            p.dma("sync", out=b1t[:, 0:hch], in_=W[f"e{e}_b1"][:, :], writes=[bb1])
            p.op("scalar", lambda s, hch=hch: s.activation(gb1[:, 0:hch], b1t[:, 0:hch], AF.Gelu),
                 reads=[bb1], writes=[bgb1])
            p.op("vector", lambda v, hch=hch: v.tensor_copy(gb1b[:, 0:hch], gb1[:, 0:hch]),
                 reads=[bgb1], writes=[bgb1])
            nsl = (hch + 7) // 8
            for sl in range(nsl):
                k0 = sl * 8
                kw = min(8, hch - k0)
                for half in range(2):
                    ws = wtog[0]; wtog[0] ^= 1
                    p.dma("gpsimd", out=wbuf[:, ws, 0:kw, 0:512],
                          in_=W[f"e{e}_w2"].rearrange("(c p) o -> p c o", p=128)[:, k0:k0+kw, half*512:(half+1)*512],
                          writes=[bwbuf[ws]])
                    Pb = P5 if half == 0 else P6
                    PBb = PB[5] if half == 0 else PB[6]
                    fns = [(lambda t, kk=kk, ws=ws, k0=k0, hch=hch, Pb=Pb: t.matmul(
                        Pb[0:1, :], gb1b[:, k0+kk:k0+kk+1], wbuf[:, ws, kk, 0:512],
                        start=(k0 == 0 and kk == 0), stop=(k0+kk == hch - 1))) for kk in range(kw)]
                    p.mm_group(fns, PBb, reads=[bgb1, bwbuf[ws]])
            for half in range(2):
                Pb = P5 if half == 0 else P6
                PBb = PB[5] if half == 0 else PB[6]
                p.op("scalar", lambda s, half=half, Pb=Pb: s.activation(
                    drowb[0:1, half*512:(half+1)*512], Pb[0:1, :], AF.Copy, scale=-1.0),
                    reads=[PBb], writes=[bdrow])
            p.dma("sync", out=dmat[e:e+1, :], in_=drowb[0:1, :], reads=[bdrow], writes=[bdmat])
        for half in range(2):
            fns = [
                (lambda t, half=half: t.matmul(P6[0:1, :], negones8[:], dmat[0:8, half*512:(half+1)*512],
                                               start=True, stop=False)),
                (lambda t, half=half: t.matmul(P6[0:1, :], ones8[:], b2b[:, half*512:(half+1)*512],
                                               start=False, stop=True)),
            ]
            p.mm_group(fns, PB[6], reads=[bdmat, bb2b, bconst])
            p.op("scalar", lambda s, half=half: s.activation(
                drowb[0:1, half*512:(half+1)*512], P6[0:1, :], AF.Copy),
                reads=[PB[6]], writes=[bdrow])
        p.dma("sync", out=dmat[8:9, :], in_=drowb[0:1, :], reads=[bdrow], writes=[bdmat])

        def ln_stats(src, srcbuf):
            fns = [(lambda t, c=c: t.matmul(P0[0:1, :], ones128[:], src[:, c, :],
                                            start=(c == 0), stop=(c == NCH-1))) for c in range(NCH)]
            p.mm_group(fns, PB[0], reads=[srcbuf, bconst])
            p.op("scalar", lambda s: s.activation(srow[:], P0[0:1, :], AF.Copy, scale=1.0/D),
                 reads=[PB[0]], writes=[brows])
            for c in range(NCH):
                p.op("scalar", lambda s, c=c: s.activation(sq[:], src[:, c, :], AF.Square),
                     reads=[srcbuf], writes=[bsq])
                p.mm_group([lambda t, c=c: t.matmul(P1[0:1, :], ones128[:], sq[:],
                                                    start=(c == 0), stop=(c == NCH-1))],
                           PB[1], reads=[bsq, bconst])
            p.op("scalar", lambda s: s.activation(qrow[:], P1[0:1, :], AF.Copy, scale=1.0/D),
                 reads=[PB[1]], writes=[brows])
            p.op("vector", lambda v: v.tensor_tensor(vrow[:], srow[:], srow[:], ALU.mult),
                 reads=[brows], writes=[brows])
            p.op("vector", lambda v: v.tensor_tensor(vrow[:], qrow[:], vrow[:], ALU.subtract),
                 reads=[brows], writes=[brows])
            p.op("scalar", lambda s: s.activation(vrow[:], vrow[:], AF.Sqrt, bias=epst[:]),
                 reads=[brows], writes=[brows])
            p.op("vector", lambda v: v.reciprocal(rrow[:], vrow[:]), reads=[brows], writes=[brows])
            p.op("vector", lambda v: v.tensor_tensor(nmr[:], srow[:], rrow[:], ALU.mult),
                 reads=[brows], writes=[brows])
            p.op("vector", lambda v: v.tensor_scalar(nmr[:], nmr[:], -1.0, None, ALU.mult),
                 reads=[brows], writes=[brows])
            p.mm_group([lambda t: t.matmul(P2[:, :], onesr[:], rrow[:], start=True, stop=True)],
                       PB[2], reads=[brows, bconst])
            p.op("scalar", lambda s: s.activation(rbcf[:], P2[:, :], AF.Copy), reads=[PB[2]], writes=[bbc])
            p.mm_group([lambda t: t.matmul(P2[:, :], onesr[:], nmr[:], start=True, stop=True)],
                       PB[2], reads=[brows, bconst])
            p.op("scalar", lambda s: s.activation(nmrbcf[:], P2[:, :], AF.Copy), reads=[PB[2]], writes=[bbc])

        for st in range(NST):
            tok0 = st * ST
            if x_dtype == F32:
                p.dma("gpsimd", out=xst[:],
                      in_=xT.rearrange("(c p) t -> p c t", p=128)[:, :, tok0:tok0+ST], writes=[bxst])
            else:
                p.dma("sync", out=xst[:],
                      in_=xT.rearrange("(c p) t -> p c t", p=128)[:, :, tok0:tok0+ST], writes=[bxst])
            ln_stats(xst, bxst)
            for c in range(NCH):
                p.op("vector", lambda v, c=c: v.tensor_tensor(sqf[:], xst[:, c, :], rbcf[:], ALU.mult),
                     reads=[bxst, bbc], writes=[bsqf])
                p.op("vector", lambda v, c=c: v.tensor_tensor(xln[:, c, :], sqf[:], nmrbcf[:], ALU.add),
                     reads=[bsqf, bbc], writes=[bxln])
                p.op("scalar", lambda s, c=c: s.activation(gxln[:, c, :], xln[:, c, :], AF.Gelu),
                     reads=[bxln], writes=[bgxln])
            for o in range(LCH0):
                if o % 4 == 0:
                    p.dma("gpsimd", out=cwb[:, :, :],
                          in_=W["down_w"].rearrange("(c p) o -> p c o", p=128)[:, :, o*128:(o+4)*128],
                          writes=[bcwb])
                osl = (o % 4) * 128
                fns = [(lambda t, k=k, osl=osl: t.matmul(P3[:, :], cwb[:, k, osl:osl+128], xln[:, k, :],
                                                     start=(k == 0), stop=(k == NCH-1)))
                       for k in range(NCH)]
                p.mm_group(fns, PB[3], reads=[bcwb, bxln])
                p.op("scalar", lambda s, o=o: s.activation(gh[:, o, :], P3[:, :], AF.Gelu,
                                                           bias=dbt[:, o:o+1]),
                     reads=[PB[3], bdb], writes=[bgh])
            for tt in range(ST // 128):
                gi = (tok0 // 128) + tt
                fns = [(lambda t, k=k, tt=tt: t.matmul(P4[0:128, 0:8], xln[:, k, tt*128:(tt+1)*128],
                                                       rwt[:, k, :], start=(k == 0), stop=False))
                       for k in range(NCH)]
                fns.append(lambda t: t.matmul(P4[0:128, 0:8], onesrb[:], rbrow[:],
                                              start=False, stop=True))
                p.mm_group(fns, PB[4], reads=[bxln, brwt, bconst])
                p.op("vector", lambda v: v.tensor_reduce(m1[:], P4[0:128, 0:8], axis=AX.X, op=ALU.max,
                                                         negate=True),
                     reads=[PB[4]], writes=[brt])
                p.op("scalar", lambda s: s.activation(pexp[:], P4[0:128, 0:8], AF.Exp, bias=m1[:],
                                                      accum_out=ssum[:]),
                     reads=[PB[4], brt], writes=[brt])
                p.op("vector", lambda v: v.reciprocal(rsum[:], ssum[:]), reads=[brt], writes=[brt])
                p.op("vector", lambda v: v.tensor_scalar(prob[:], pexp[:], rsum[:], None, ALU.mult),
                     reads=[brt], writes=[brt])
                p.op("vector", lambda v: v.tensor_reduce(v1t[:], prob[:], axis=AX.X, op=ALU.max),
                     reads=[brt], writes=[brt])
                p.op("vector", lambda v: v.tensor_scalar(mask1[:], prob[:], v1t[:], None, ALU.is_ge),
                     reads=[brt], writes=[brt])
                p.op("vector", lambda v: v.tensor_tensor(ptmp[:], prob[:], mask1[:], ALU.mult),
                     reads=[brt], writes=[brt])
                p.op("vector", lambda v: v.tensor_tensor(p2t[:], prob[:], ptmp[:], ALU.subtract),
                     reads=[brt], writes=[brt])
                p.op("vector", lambda v: v.tensor_reduce(v2t[:], p2t[:], axis=AX.X, op=ALU.max),
                     reads=[brt], writes=[brt])
                p.op("vector", lambda v: v.tensor_scalar(mask2[:], p2t[:], v2t[:], None, ALU.is_ge),
                     reads=[brt], writes=[brt])
                p.op("vector", lambda v: v.tensor_copy(evt[:, 0:1], v1t[:]), reads=[brt], writes=[brt])
                p.op("vector", lambda v: v.tensor_copy(evt[:, 1:2], v2t[:]), reads=[brt], writes=[brt])
                p.op("scalar", lambda s: s.activation(evt[:], evt[:], AF.Exp), reads=[brt], writes=[brt])
                p.op("vector", lambda v: v.tensor_reduce(dent[:], evt[:], axis=AX.X, op=ALU.add),
                     reads=[brt], writes=[brt])
                p.op("vector", lambda v: v.reciprocal(rdent[:], dent[:]), reads=[brt], writes=[brt])
                p.op("vector", lambda v: v.tensor_scalar(g12[:], evt[:], rdent[:], None, ALU.mult),
                     reads=[brt], writes=[brt])
                p.op("vector", lambda v: v.tensor_scalar(lam[:], mask1[:], g12[:, 0:1], None, ALU.mult),
                     reads=[brt], writes=[blam])
                p.op("vector", lambda v: v.scalar_tensor_tensor(lam[:], mask2[:], g12[:, 1:2], lam[:],
                                                                ALU.mult, ALU.add),
                     reads=[brt, blam], writes=[blam])
                p.op("tensor", lambda t: t.transpose(P5[0:8, 0:128], lam[:, 0:8], identt[:]),
                     reads=[blam, bconst], writes=[PB[5]])
                p.op("scalar", lambda s, gi=gi: s.activation(lamT[:, gi, :], P5[0:8, 0:128], AF.Copy),
                     reads=[PB[5]], writes=[blamT])
            p.op("vector", lambda v: v.memset(lamb9[:], 1.0), writes=[blam9])
            p.op("vector", lambda v, st=st: v.tensor_copy(
                lamb9[0:8, :], lamT[:, st*4:(st+1)*4, :].rearrange("p a b -> p (a b)")),
                 reads=[blamT], writes=[blam9])

            for e in range(8):
                hch = HID0[e] // 128
                p.dma("sync", out=lrow[:], in_=lamT[e:e+1, st*4:(st+1)*4, :].rearrange("p a b -> p (a b)"),
                      reads=[blamT], writes=[blrow])
                p.mm_group([lambda t: t.matmul(P5[:, :], onesrb[:], lrow[:], start=True, stop=True)],
                           PB[5], reads=[blrow, bconst])
                p.op("scalar", lambda s: s.activation(lbc[:], P5[:, :], AF.Copy), reads=[PB[5]], writes=[blbc])
                p.dma("sync", out=b1t[:, 0:hch], in_=W[f"e{e}_b1"][:, :], writes=[bb1])
                nhalf = (hch + 7) // 8
                for half in range(nhalf):
                    h0 = half * 8
                    hw = min(8, hch - h0)
                    for q2 in range(2):
                        qh0 = h0 + q2 * 4
                        qw = min(4, hch - qh0)
                        if qw <= 0:
                            continue
                        ws = wtog[0]; wtog[0] ^= 1
                        p.dma("gpsimd", out=wbuf[:, ws, 0:LCH0, 0:qw*128],
                              in_=W[f"e{e}_w1"].rearrange("(c p) o -> p c o", p=128)[:, :, qh0*128:(qh0+qw)*128],
                              writes=[bwbuf[ws]])
                        for o in range(qw):
                            oc = q2 * 4 + o
                            bank = 3 if oc % 2 == 0 else 4
                            fns = [(lambda t, k=k, o=o, bank=bank, ws=ws: t.matmul(
                                PS[bank][:, :], wbuf[:, ws, k, o*128:(o+1)*128], gh[:, k, :],
                                start=(k == 0), stop=(k == LCH0-1))) for k in range(LCH0)]
                            p.mm_group(fns, PB[bank], reads=[bwbuf[ws], bgh])
                            p.op("scalar", lambda s, oc=oc, bank=bank, qh0=qh0, o=o: s.activation(
                                gtile[:, oc, :], PS[bank][:, :], AF.Gelu, bias=b1t[:, qh0+o:qh0+o+1]),
                                 reads=[PB[bank], bb1], writes=[bgt])
                            p.op("vector", lambda v, oc=oc: v.tensor_tensor(
                                gtile[:, oc, :], gtile[:, oc, :], lbc[:], ALU.mult),
                                 reads=[bgt, blbc], writes=[bgt])
                    for q2 in range(2):
                        ws = wtog[0]; wtog[0] ^= 1
                        p.dma("gpsimd", out=wbuf[:, ws, 0:hw, 0:512],
                              in_=W[f"e{e}_w2"].rearrange("(c p) o -> p c o", p=128)[:, h0:h0+hw, q2*512:(q2+1)*512],
                              writes=[bwbuf[ws]])
                        for F4 in range(4):
                            F = q2 * 4 + F4
                            bank = 3 if F % 2 == 0 else 4
                            fns = [(lambda t, k=k, F4=F4, bank=bank, hw=hw, ws=ws: t.matmul(
                                PS[bank][:, :], wbuf[:, ws, k, F4*128:(F4+1)*128], gtile[:, k, :],
                                start=(k == 0), stop=(k == hw-1))) for k in range(hw)]
                            p.mm_group(fns, PB[bank], reads=[bwbuf[ws], bgt])
                            if e == 0 and half == 0:
                                p.op("scalar", lambda s, F=F, bank=bank: s.activation(
                                    out2[:, F, :], PS[bank][:, :], AF.Copy),
                                     reads=[PB[bank]], writes=[bout2])
                            else:
                                p.op("vector", lambda v, F=F, bank=bank: v.tensor_tensor(
                                    out2[:, F, :], out2[:, F, :], PS[bank][:, :], ALU.add),
                                     reads=[PB[bank], bout2], writes=[bout2])
            for o in range(LCH0):
                bank = 3 if o % 2 == 0 else 4
                fns = [(lambda t, k=k, o=o, bank=bank: t.matmul(
                    PS[bank][:, :], sh1t[:, k, o*128:(o+1)*128], gh[:, k, :],
                    start=(k == 0), stop=(k == LCH0-1))) for k in range(LCH0)]
                p.mm_group(fns, PB[bank], reads=[bsh1, bgh])
                p.op("scalar", lambda s, o=o, bank=bank: s.activation(
                    gtile[:, o, :], PS[bank][:, :], AF.Gelu, bias=sb1c[:, o:o+1]),
                     reads=[PB[bank], bsb1], writes=[bgt])
            for F in range(LCH0):
                bank = 3 if F % 2 == 0 else 4
                fns = [(lambda t, k=k, F=F, bank=bank: t.matmul(
                    PS[bank][:, :], sh2t[:, k, F*128:(F+1)*128], gtile[:, k, :],
                    start=(k == 0), stop=(k == LCH0-1))) for k in range(LCH0)]
                p.mm_group(fns, PB[bank], reads=[bsh2, bgt])
                p.mm_group([(lambda t, F=F: t.matmul(
                    P5[:, :], dmat[:, F*128:(F+1)*128], lamb9[:], start=True, stop=True))],
                    PB[5], reads=[bdmat, blam9])
                p.op("scalar", lambda s, F=F, bank=bank: s.activation(
                    sqf[:], PS[bank][:, :], AF.Identity, bias=shb2c[:, F:F+1]),
                     reads=[PB[bank], bshb2], writes=[bsqf])
                p.op("vector", lambda v, F=F: v.scalar_tensor_tensor(
                    out2[:, F, :], sqf[:], 0.1, out2[:, F, :], ALU.mult, ALU.add),
                     reads=[bsqf, bout2], writes=[bout2])
                p.op("vector", lambda v, F=F: v.tensor_tensor(
                    out2[:, F, :], out2[:, F, :], P5[:, :], ALU.add),
                     reads=[PB[5], bout2], writes=[bout2])
            for F in range(LCH0):
                p.op("vector", lambda v, F=F: v.tensor_copy(out2b[:, F, :], out2[:, F, :]),
                     reads=[bout2], writes=[bout2b])
            for dchunk in range(NCH):
                bank = 3 if dchunk % 2 == 0 else 4
                if dchunk % 4 == 0:
                    p.dma("gpsimd", out=cwb[:, :, :],
                          in_=W["core_w"].rearrange("(c p) o -> p c o", p=128)[:, :, dchunk*128:(dchunk+4)*128],
                          writes=[bcwb])
                    p.dma("gpsimd", out=upt[:, :, :],
                          in_=W["up_w"].rearrange("(c p) o -> p c o", p=128)[:, :, dchunk*128:(dchunk+4)*128],
                          writes=[bupt])
                sl = (dchunk % 4) * 128
                fns = [(lambda t, k=k, sl=sl, bank=bank: t.matmul(
                    PS[bank][:, :], upt[:, k, sl:sl+128], out2b[:, k, :],
                    start=(k == 0), stop=False)) for k in range(LCH0)]
                fns += [(lambda t, k=k, sl=sl, bank=bank: t.matmul(
                    PS[bank][:, :], cwb[:, k, sl:sl+128], gxln[:, k, :],
                    start=False, stop=(k == NCH-1))) for k in range(NCH)]
                p.mm_group(fns, PB[bank], reads=[bupt, bout2b, bcwb, bgxln])
                p.op("scalar", lambda s, dchunk=dchunk, bank=bank: s.activation(
                    ffno[:, dchunk, :], PS[bank][:, :], AF.Identity, bias=biasc[:, dchunk:dchunk+1]),
                     reads=[PB[bank], bbias], writes=[bffno])
            ln_stats(ffno, bffno)
            for c in range(NCH):
                p.dma("sync", out=xf32[:],
                      in_=xT.rearrange("(c p) t -> p c t", p=128)[:, c, tok0:tok0+ST],
                      writes=[bxf32])
                p.op("vector", lambda v, c=c: v.tensor_tensor(sqf[:], ffno[:, c, :], rbcf[:], ALU.mult),
                     reads=[bffno, bbc], writes=[bsqf])
                p.op("vector", lambda v, c=c: v.tensor_tensor(sqf[:], sqf[:], nmrbcf[:], ALU.add),
                     reads=[bsqf, bbc], writes=[bsqf])
                p.op("vector", lambda v, c=c: v.tensor_tensor(xnf[:], xf32[:], sqf[:], ALU.add),
                     reads=[bxf32, bsqf], writes=[bxnf])
                p.dma("gpsimd", out=xO.rearrange("(c p) t -> p c t", p=128)[:, c, tok0:tok0+ST],
                      in_=xnf[:], reads=[bxnf], writes=[bxO])
        p.finish([bxO])
        p.emit(block)
    es.close()
    return nc, p




D = 2048
NTOK = 2048
ST = 512
NST = NTOK // ST
NCH = D // 128
T = 4096
LAMBDA_INIT = 0.8 - 0.6 * float(np.exp(-0.3))


def _ln_stats_factory(p, nc, env):
    (ones128, onesr, epst, srow, qrow, vrow, rrow, nmr, rbcf, nmrbcf, sq,
     P0, P1, P2, PB, bconst, brows, bbc, bsq) = env
    def ln_stats(src, srcbuf):
        fns = [(lambda t, c=c: t.matmul(P0[0:1, :], ones128[:], src[:, c, :],
                                        start=(c == 0), stop=(c == NCH-1))) for c in range(NCH)]
        p.mm_group(fns, PB[0], reads=[srcbuf, bconst])
        p.op("scalar", lambda s: s.activation(srow[:], P0[0:1, :], AF.Copy, scale=1.0/D),
             reads=[PB[0]], writes=[brows])
        for c in range(NCH):
            p.op("scalar", lambda s, c=c: s.activation(sq[:], src[:, c, :], AF.Square),
                 reads=[srcbuf], writes=[bsq])
            p.mm_group([lambda t, c=c: t.matmul(P1[0:1, :], ones128[:], sq[:],
                                                start=(c == 0), stop=(c == NCH-1))],
                       PB[1], reads=[bsq, bconst])
        p.op("scalar", lambda s: s.activation(qrow[:], P1[0:1, :], AF.Copy, scale=1.0/D),
             reads=[PB[1]], writes=[brows])
        p.op("vector", lambda v: v.tensor_tensor(vrow[:], srow[:], srow[:], ALU.mult),
             reads=[brows], writes=[brows])
        p.op("vector", lambda v: v.tensor_tensor(vrow[:], qrow[:], vrow[:], ALU.subtract),
             reads=[brows], writes=[brows])
        p.op("scalar", lambda s: s.activation(vrow[:], vrow[:], AF.Sqrt, bias=epst[:]),
             reads=[brows, bconst], writes=[brows])
        p.op("vector", lambda v: v.reciprocal(rrow[:], vrow[:]), reads=[brows], writes=[brows])
        p.op("vector", lambda v: v.tensor_tensor(nmr[:], srow[:], rrow[:], ALU.mult),
             reads=[brows], writes=[brows])
        p.op("vector", lambda v: v.tensor_scalar(nmr[:], nmr[:], -1.0, None, ALU.mult),
             reads=[brows], writes=[brows])
        p.mm_group([lambda t: t.matmul(P2[:, :], onesr[:], rrow[:], start=True, stop=True)],
                   PB[2], reads=[brows, bconst])
        p.op("scalar", lambda s: s.activation(rbcf[:], P2[:, :], AF.Copy), reads=[PB[2]], writes=[bbc])
        p.mm_group([lambda t: t.matmul(P2[:, :], onesr[:], nmr[:], start=True, stop=True)],
                   PB[2], reads=[brows, bbc, bconst])
        p.op("scalar", lambda s: s.activation(nmrbcf[:], P2[:, :], AF.Copy), reads=[PB[2]], writes=[bbc])
    return ln_stats


def build_l1p1(x_dtype=BF16):
    nc = bass.Bass("TRN2", target_bir_lowering=False)
    xT = nc.dram_tensor("xT", [D, NTOK], x_dtype, kind="ExternalInput").ap()
    wq = nc.dram_tensor("w_qkv", [D, 3 * D], F32, kind="ExternalInput").ap()
    qkvO = nc.dram_tensor("qkvO", [3 * D, NTOK], F32, kind="ExternalOutput").ap()
    from contextlib import ExitStack
    es = ExitStack()
    Tn = lambda *a: es.enter_context(nc.sbuf_tensor(*a))
    xst = Tn("xst", [128, NCH, ST], BF16)
    xln = Tn("xln", [128, NCH, ST], BF16)
    sq = Tn("sq", [128, ST], BF16)
    sqf = Tn("sqf", [128, ST], F32)
    wsl = Tn("wsl", [128, 2, 16, 512], BF16)
    outt = Tn("outt", [128, 2, 4, ST], F32)
    ones128 = Tn("ones128", [128, 1], BF16)
    onesr = Tn("onesr", [1, 128], F32)
    epst = Tn("epst", [1, 1], F32)
    srow = Tn("srow", [1, ST], F32); qrow = Tn("qrow", [1, ST], F32)
    vrow = Tn("vrow", [1, ST], F32); rrow = Tn("rrow", [1, ST], F32)
    nmr = Tn("nmr", [1, ST], F32)
    rbcf = Tn("rbcf", [128, ST], F32); nmrbcf = Tn("nmrbcf", [128, ST], F32)
    P0 = es.enter_context(nc.psum_tensor("P0", [128, 512], F32))
    P1p = es.enter_context(nc.psum_tensor("P1", [128, 512], F32))
    P2p = es.enter_context(nc.psum_tensor("P2", [128, 512], F32))
    P3p = es.enter_context(nc.psum_tensor("P3", [128, 512], F32))
    P4p = es.enter_context(nc.psum_tensor("P4", [128, 512], F32))
    with nc.Block() as block:
        p = Prog(nc); p.setup(dma_slots={"sync": 3, "gpsimd": 3})
        PB = [Buf(f"P{i}") for i in range(5)]
        bxst, bxln, bsq, bsqf = (Buf(n) for n in ("xst", "xln", "sq", "sqf"))
        bwsl = [Buf("wsl0"), Buf("wsl1")]
        bout = [Buf("out0"), Buf("out1")]
        bconst, brows, bbc, bO = Buf("const"), Buf("rows"), Buf("bc"), Buf("O")
        p.op("vector", lambda v: v.memset(ones128[:], 1.0), writes=[bconst])
        p.op("vector", lambda v: v.memset(onesr[:], 1.0), writes=[bconst])
        p.op("vector", lambda v: v.memset(epst[:], 1e-5), writes=[bconst])
        ln_stats = _ln_stats_factory(p, nc, (ones128, onesr, epst, srow, qrow, vrow, rrow, nmr,
                                             rbcf, nmrbcf, sq, P0, P1p, P2p, PB, bconst, brows, bbc, bsq))
        wqa = wq.rearrange("(c p) o -> p c o", p=128)
        for st in range(NST):
            tok0 = st * ST
            q = "gpsimd" if x_dtype == F32 else "sync"
            p.dma(q, out=xst[:], in_=xT.rearrange("(c p) t -> p c t", p=128)[:, :, tok0:tok0+ST],
                  writes=[bxst])
            ln_stats(xst, bxst)
            for c in range(NCH):
                p.op("vector", lambda v, c=c: v.tensor_tensor(sqf[:], xst[:, c, :], rbcf[:], ALU.mult),
                     reads=[bxst, bbc], writes=[bsqf])
                p.op("vector", lambda v, c=c: v.tensor_tensor(xln[:, c, :], sqf[:], nmrbcf[:], ALU.add),
                     reads=[bsqf, bbc], writes=[bxln])
            for og in range(12):
                ws = og % 2
                p.dma("gpsimd", out=wsl[:, ws, :, :], in_=wqa[:, :, og*512:(og+1)*512],
                      writes=[bwsl[ws]])
                for o in range(4):
                    bank = 3 if o % 2 == 0 else 4
                    Pb = P3p if bank == 3 else P4p
                    fns = [(lambda t, k=k, o=o, Pb=Pb, ws=ws: t.matmul(
                        Pb[:, :], wsl[:, ws, k, o*128:(o+1)*128], xln[:, k, :],
                        start=(k == 0), stop=(k == NCH-1))) for k in range(NCH)]
                    p.mm_group(fns, PB[bank], reads=[bwsl[ws], bxln])
                    p.op("scalar", lambda s, o=o, bank=bank, Pb=Pb, ws=ws: s.activation(
                        outt[:, ws, o, :], Pb[:, :], AF.Copy),
                         reads=[PB[bank]], writes=[bout[ws]])
                p.dma("sync", out=qkvO.rearrange("(c p) t -> p c t", p=128)[:, og*4:(og+1)*4, tok0:tok0+ST],
                      in_=outt[:, ws, :, :], reads=[bout[ws]], writes=[bO])
        p.finish([bO])
        p.emit(block)
    es.close()
    return nc, p


def build_l1p2():
    nc = bass.Bass("TRN2", target_bir_lowering=False)
    Q1 = nc.dram_tensor("Q1", [512, T], F32, kind="ExternalInput").ap()
    Q2 = nc.dram_tensor("Q2", [512, T], F32, kind="ExternalInput").ap()
    K1 = nc.dram_tensor("K1", [512, T], F32, kind="ExternalInput").ap()
    K2 = nc.dram_tensor("K2", [512, T], F32, kind="ExternalInput").ap()
    V = nc.dram_tensor("V", [1024, T], F32, kind="ExternalInput").ap()
    LV = nc.dram_tensor("LV", [64, 4], F32, kind="ExternalInput").ap()
    ident = nc.dram_tensor("ident", [128, 128], F32, kind="ExternalInput").ap()
    ON = nc.dram_tensor("ON", [1024, T], F32, kind="ExternalOutput").ap()
    from contextlib import ExitStack
    es = ExitStack()
    Tn = lambda *a: es.enter_context(nc.sbuf_tensor(*a))
    qf = Tn("qf", [64, T], F32)
    q1 = Tn("q1", [64, T], BF16); q2 = Tn("q2", [64, T], BF16)
    k1 = Tn("k1", [64, T], BF16); k2 = Tn("k2", [64, T], BF16)
    vfm = Tn("vfm", [128, T], BF16)
    vtm = Tn("vtm", [128, 32, 128], BF16)
    ptil = Tn("ptil", [128, 2, 512], BF16)
    osb = Tn("osb", [128, 2, 2, 512], F32)
    d0row = Tn("d0row", [1, 2, 512], F32)
    d1row = Tn("d1row", [1, 2, 512], F32)
    dcol = Tn("dcol", [128, 2], F32)
    rdcol = Tn("rdcol", [128, 2], F32)
    n1t = Tn("n1t", [128, 128], F32)
    n2t = Tn("n2t", [128, 128], F32)
    opre = Tn("opre", [128, 4, 128], F32)
    ostF = Tn("ostF", [128, 4, 128], F32)
    ssq = Tn("ssq", [128, 1], F32)
    eps6 = Tn("eps6", [128, 1], F32)
    rstd = Tn("rstd", [128, 1], F32)
    lamv = Tn("lamv", [64, 4], F32)
    lam1 = Tn("lam1", [1, 2], F32)
    lamc = Tn("lamc", [1, 1], F32)
    lam128 = Tn("lam128", [128, 1], F32)
    onesrf = Tn("onesrf", [1, 128], F32)
    identt = Tn("identt", [128, 128], F32)
    identb = Tn("identb", [128, 128], BF16)
    ones128 = Tn("ones128", [128, 1], BF16)
    P0 = es.enter_context(nc.psum_tensor("P0", [128, 512], F32))
    P1p = es.enter_context(nc.psum_tensor("P1", [128, 512], F32))
    P2p = es.enter_context(nc.psum_tensor("P2", [128, 512], F32))
    P3p = es.enter_context(nc.psum_tensor("P3", [128, 512], F32))
    P4p = es.enter_context(nc.psum_tensor("P4", [128, 512], F32))
    P5p = es.enter_context(nc.psum_tensor("P5", [128, 512], F32))
    P6p = es.enter_context(nc.psum_tensor("P6", [128, 512], BF16))
    P7p = es.enter_context(nc.psum_tensor("P7", [128, 512], F32))
    with nc.Block() as block:
        p = Prog(nc); p.setup(dma_slots={"sync": 3, "gpsimd": 3})
        bq1, bq2, bk1, bk2, bv, bvtm = (Buf(n) for n in ("q1", "q2", "k1", "k2", "v", "vtm"))
        bptil = [Buf("ptil0"), Buf("ptil1")]
        bosb = [Buf("osb0"), Buf("osb1")]
        bdrows = [Buf("drows0"), Buf("drows1")]
        bdcol, brd = Buf("dcol"), Buf("rd")
        bn1, bn2, bopre, bonet, bssq = (Buf(n) for n in ("n1", "n2", "opre", "onet", "ssq"))
        blam, bconst, bON = Buf("lam"), Buf("const"), Buf("ON")
        BP = {n: Buf(n) for n in ("P0", "P1", "P2", "P3", "P4", "P5", "P6", "P7")}
        p.op("vector", lambda v: v.memset(ones128[:], 1.0), writes=[bconst])
        p.op("vector", lambda v: v.memset(onesrf[:], 1.0), writes=[bconst])
        p.op("vector", lambda v: v.memset(eps6[:], 1e-6), writes=[bconst])
        p.dma("sync", out=identt[:], in_=ident[:, :], writes=[bconst])
        p.op("vector", lambda v: v.tensor_copy(identb[:], identt[:]), reads=[bconst], writes=[bconst])
        p.dma("sync", out=lamv[:], in_=LV[:, :], writes=[blam])
        p.mm_group([lambda t: t.matmul(P4p[0:1, 0:1], lamv[:, 0:1], lamv[:, 1:2], start=True, stop=True)],
                   BP["P4"], reads=[blam])
        p.op("scalar", lambda s: s.activation(lam1[0:1, 0:1], P4p[0:1, 0:1], AF.Exp),
             reads=[BP["P4"]], writes=[blam])
        p.mm_group([lambda t: t.matmul(P4p[0:1, 0:1], lamv[:, 2:3], lamv[:, 3:4], start=True, stop=True)],
                   BP["P4"], reads=[blam])
        p.op("scalar", lambda s: s.activation(lam1[0:1, 1:2], P4p[0:1, 0:1], AF.Exp),
             reads=[BP["P4"]], writes=[blam])
        p.op("vector", lambda v: v.tensor_tensor(lamc[:], lam1[0:1, 0:1], lam1[0:1, 1:2], ALU.subtract),
             reads=[blam], writes=[blam])
        p.op("vector", lambda v: v.tensor_scalar(lamc[:], lamc[:], LAMBDA_INIT, None, ALU.add),
             reads=[blam], writes=[blam])
        p.mm_group([lambda t: t.matmul(P5p[:, 0:1], onesrf[:], lamc[:], start=True, stop=True)],
                   BP["P5"], reads=[blam, bconst])
        p.op("scalar", lambda s: s.activation(lam128[:], P5p[:, 0:1], AF.Copy),
             reads=[BP["P5"]], writes=[blam])

        SCALE = 1.0 / 8.0

        def epilogue_a(sl):
            for qt in range(4):
                p.op("tensor", lambda t, sl=sl, qt=qt: t.transpose(
                    P7p[:, 0:128], osb[:, sl, 0, qt*128:(qt+1)*128], identt[:]),
                     reads=[bosb[sl], bconst], writes=[BP["P7"]])
                p.op("scalar", lambda s: s.activation(n1t[:], P7p[:, 0:128], AF.Copy),
                     reads=[BP["P7"]], writes=[bn1])
                p.op("tensor", lambda t, sl=sl, qt=qt: t.transpose(
                    P7p[:, 0:128], osb[:, sl, 1, qt*128:(qt+1)*128], identt[:]),
                     reads=[bosb[sl], bconst], writes=[BP["P7"]])
                p.op("scalar", lambda s: s.activation(n2t[:], P7p[:, 0:128], AF.Copy),
                     reads=[BP["P7"]], writes=[bn2])
                p.op("tensor", lambda t, sl=sl, qt=qt: t.transpose(
                    P7p[0:128, 0:1], d0row[0:1, sl, qt*128:(qt+1)*128], identt[0:1, 0:1]),
                     reads=[bdrows[sl], bconst], writes=[BP["P7"]])
                p.op("scalar", lambda s: s.activation(dcol[:, 0:1], P7p[0:128, 0:1], AF.Copy),
                     reads=[BP["P7"]], writes=[bdcol])
                p.op("tensor", lambda t, sl=sl, qt=qt: t.transpose(
                    P7p[0:128, 0:1], d1row[0:1, sl, qt*128:(qt+1)*128], identt[0:1, 0:1]),
                     reads=[bdrows[sl], bdcol, bconst], writes=[BP["P7"]])
                p.op("scalar", lambda s: s.activation(dcol[:, 1:2], P7p[0:128, 0:1], AF.Copy),
                     reads=[BP["P7"]], writes=[bdcol])
                p.op("vector", lambda v: v.reciprocal(rdcol[:], dcol[:]), reads=[bdcol], writes=[brd])
                p.op("vector", lambda v, qt=qt: v.tensor_scalar(opre[:, qt, :], n1t[:], rdcol[:, 0:1],
                                                                None, ALU.mult),
                     reads=[bn1, brd], writes=[bopre])
                p.op("vector", lambda v: v.tensor_scalar(n2t[:], n2t[:], rdcol[:, 1:2],
                                                         None, ALU.mult),
                     reads=[bn2, brd], writes=[bn2])
                p.op("vector", lambda v: v.tensor_scalar(n2t[:], n2t[:], lam128[:],
                                                         None, ALU.mult),
                     reads=[bn2, blam], writes=[bn2])
                p.op("vector", lambda v, qt=qt: v.tensor_tensor(opre[:, qt, :], opre[:, qt, :],
                                                                n2t[:], ALU.subtract),
                     reads=[bopre, bn2], writes=[bopre])
                p.op("scalar", lambda s, qt=qt: s.activation(n1t[:], opre[:, qt, :], AF.Square,
                                                             accum_out=ssq[:]),
                     reads=[bopre], writes=[bssq, bn1])
                p.op("scalar", lambda s: s.activation(rstd[:], ssq[:], AF.Sqrt, scale=1.0/128.0,
                                                      bias=eps6[:]),
                     reads=[bssq, bconst], writes=[bssq])
                p.op("vector", lambda v: v.reciprocal(rstd[:], rstd[:]), reads=[bssq], writes=[bssq])
                p.op("vector", lambda v, qt=qt: v.tensor_scalar(opre[:, qt, :], opre[:, qt, :],
                                                                rstd[:], None, ALU.mult),
                     reads=[bopre, bssq], writes=[bopre])

        def epilogue_b(u, qb):
            for qt in range(4):
                p.op("tensor", lambda t, qt=qt: t.transpose(P7p[:, 0:128], opre[:, qt, :], identt[:]),
                     reads=[bopre, bconst], writes=[BP["P7"]])
                p.op("scalar", lambda s, qt=qt: s.activation(ostF[:, qt, :], P7p[:, 0:128],
                                                             AF.Copy),
                     reads=[BP["P7"]], writes=[bonet])
            p.dma("gpsimd", out=ON[u*128:(u+1)*128, qb*512:(qb+1)*512].rearrange(
                      "p (f w) -> p f w", f=4),
                  in_=ostF[:], reads=[bonet], writes=[bON])

        pend = [None]
        for u in range(8):
            bqf = BP.setdefault("qf", Buf("qf"))
            for (DD, tile, btile) in ((Q1, q1, bq1), (Q2, q2, bq2), (K1, k1, bk1), (K2, k2, bk2)):
                p.dma("sync", out=qf[:], in_=DD[u*64:(u+1)*64, :], writes=[bqf])
                p.op("vector", lambda v, tile=tile: v.tensor_copy(tile[:], qf[:]),
                     reads=[bqf], writes=[btile])
            p.dma("gpsimd", out=vfm[:], in_=V[u*128:(u+1)*128, :], writes=[bv])
            for kc in range(32):
                p.op("tensor", lambda t, kc=kc: t.transpose(P6p[:, 0:128], vfm[:, kc*128:(kc+1)*128],
                                                            identb[:]),
                     reads=[bv, bconst], writes=[BP["P6"]])
                p.op("scalar", lambda s, kc=kc: s.activation(vtm[:, kc, :], P6p[:, 0:128], AF.Copy),
                     reads=[BP["P6"]], writes=[bvtm])
            for qb in range(8):
                qs = slice(qb*512, (qb+1)*512)
                sl = qb % 2
                for sm, (qq, kk, bqq, bkk, Po, bPo, Pd, bPd) in enumerate((
                        (q1, k1, bq1, bk1, P2p, BP["P2"], P4p, BP["P4"]),
                        (q2, k2, bq2, bk2, P3p, BP["P3"], P5p, BP["P5"]))):
                    for kc in range(32):
                        Ps = P0 if kc % 2 == 0 else P1p
                        bPs = BP["P0"] if kc % 2 == 0 else BP["P1"]
                        pts = kc % 2
                        p.mm_group([
                            (lambda t, kc=kc, qs=qs, Ps=Ps, qq=qq, kk=kk: t.matmul(
                                Ps[:, :], kk[:, kc*128:(kc+1)*128], qq[:, qs],
                                start=True, stop=True)),
                        ], bPs, reads=[bqq, bkk])
                        p.op("scalar", lambda s, Ps=Ps, pts=pts: s.activation(
                            ptil[:, pts, :], Ps[:, :], AF.Exp, scale=SCALE),
                             reads=[bPs], writes=[bptil[pts]])
                        p.mm_group([(lambda t, kc=kc, Po=Po, pts=pts: t.matmul(
                            Po[:, :], vtm[:, kc, :], ptil[:, pts, :],
                            start=(kc == 0), stop=(kc == 31)))], bPo, reads=[bvtm, bptil[pts]])
                        p.mm_group([(lambda t, kc=kc, Pd=Pd, pts=pts: t.matmul(
                            Pd[0:1, :], ones128[:], ptil[:, pts, :],
                            start=(kc == 0), stop=(kc == 31)))], bPd, reads=[bptil[pts], bconst])
                    p.op("scalar", lambda s, sl=sl, sm=sm, Po=Po: s.activation(
                        osb[:, sl, sm, :], Po[:, :], AF.Copy),
                         reads=[bPo], writes=[bosb[sl]])
                    drow_t = d0row if sm == 0 else d1row
                    p.op("scalar", lambda s, sl=sl, Pd=Pd, drow_t=drow_t: s.activation(
                        drow_t[0:1, sl, :], Pd[0:1, :], AF.Copy),
                         reads=[bPd], writes=[bdrows[sl]])
                if pend[0] is not None:
                    epilogue_b(*pend[0])
                epilogue_a(sl)
                pend[0] = (u, qb)
        if pend[0] is not None:
            epilogue_b(*pend[0])
        p.finish([bON])
        p.emit(block)
    es.close()
    return nc, p


def build_l1p3():
    nc = bass.Bass("TRN2", target_bir_lowering=False)
    xT = nc.dram_tensor("xT", [D, NTOK], F32, kind="ExternalInput").ap()
    onFd = nc.dram_tensor("onF", [D, NTOK], F32, kind="ExternalInput").ap()
    wp = nc.dram_tensor("w_proj", [D, D], F32, kind="ExternalInput").ap()
    rs = nc.dram_tensor("rms_scale", [128, 1], F32, kind="ExternalInput").ap()
    xO = nc.dram_tensor("xO", [D, NTOK], F32, kind="ExternalOutput").ap()
    from contextlib import ExitStack
    es = ExitStack()
    Tn = lambda *a: es.enter_context(nc.sbuf_tensor(*a))
    xst = Tn("xst", [128, NCH, ST], BF16)
    ont = Tn("ont", [128, NCH, ST], F32)
    ontb = Tn("ontb", [128, NCH, ST], BF16)
    xf32 = Tn("xf32", [128, 512], F32)
    wsl = Tn("wsl", [128, 2, 16, 512], BF16)
    ffno = Tn("ffno", [128, NCH, ST], BF16)
    sq = Tn("sq", [128, ST], BF16)
    sqf = Tn("sqf", [128, ST], F32)
    rst = Tn("rst", [128, 1], F32)
    ones128 = Tn("ones128", [128, 1], BF16)
    onesr = Tn("onesr", [1, 128], F32)
    epst = Tn("epst", [1, 1], F32)
    srow = Tn("srow", [1, ST], F32); qrow = Tn("qrow", [1, ST], F32)
    vrow = Tn("vrow", [1, ST], F32); rrow = Tn("rrow", [1, ST], F32)
    nmr = Tn("nmr", [1, ST], F32)
    rbcf = Tn("rbcf", [128, ST], F32); nmrbcf = Tn("nmrbcf", [128, ST], F32)
    P0 = es.enter_context(nc.psum_tensor("P0", [128, 512], F32))
    P1p = es.enter_context(nc.psum_tensor("P1", [128, 512], F32))
    P2p = es.enter_context(nc.psum_tensor("P2", [128, 512], F32))
    P3p = es.enter_context(nc.psum_tensor("P3", [128, 512], F32))
    P4p = es.enter_context(nc.psum_tensor("P4", [128, 512], F32))
    with nc.Block() as block:
        p = Prog(nc); p.setup(dma_slots={"sync": 3, "gpsimd": 3})
        PB = [Buf(f"P{i}") for i in range(5)]
        bxst, bont, bffno, bxf32 = (Buf(n) for n in ("xst", "ont", "ffno", "xf32"))
        bwsl = [Buf("wsl0"), Buf("wsl1")]
        bsq, bsqf, bconst, brows, bbc, bO, brs = (Buf(n) for n in
            ("sq", "sqf", "const", "rows", "bc", "O", "rs"))
        p.op("vector", lambda v: v.memset(ones128[:], 1.0), writes=[bconst])
        p.op("vector", lambda v: v.memset(onesr[:], 1.0), writes=[bconst])
        p.op("vector", lambda v: v.memset(epst[:], 1e-5), writes=[bconst])
        p.dma("sync", out=rst[:], in_=rs[:, :], writes=[brs])
        p.op("vector", lambda v: v.tensor_scalar(rst[:], rst[:], 1.0 - LAMBDA_INIT, None, ALU.mult),
             reads=[brs], writes=[brs])
        ln_stats = _ln_stats_factory(p, nc, (ones128, onesr, epst, srow, qrow, vrow, rrow, nmr,
                                             rbcf, nmrbcf, sq, P0, P1p, P2p, PB, bconst, brows, bbc, bsq))
        wpa = wp.rearrange("(c p) o -> p c o", p=128)
        for st in range(NST):
            tok0 = st * ST
            p.dma("gpsimd", out=xst[:], in_=xT.rearrange("(c p) t -> p c t", p=128)[:, :, tok0:tok0+ST],
                  writes=[bxst])
            p.dma("sync", out=ont[:], in_=onFd.rearrange("(c p) t -> p c t", p=128)[:, :, tok0:tok0+ST],
                  writes=[bont])
            for c in range(NCH):
                p.op("vector", lambda v, c=c: v.tensor_copy(ontb[:, c, :], ont[:, c, :]),
                     reads=[bont], writes=[bont])
            for og in range(4):
                ws = og % 2
                p.dma("gpsimd", out=wsl[:, ws, :, :], in_=wpa[:, :, og*512:(og+1)*512],
                      writes=[bwsl[ws]])
                for c in range(NCH):
                    p.op("vector", lambda v, c=c, ws=ws: v.tensor_scalar(
                        wsl[:, ws, c, :], wsl[:, ws, c, :], rst[:], None, ALU.mult),
                         reads=[bwsl[ws], brs], writes=[bwsl[ws]])
                for o in range(4):
                    bank = 3 if o % 2 == 0 else 4
                    Pb = P3p if bank == 3 else P4p
                    fns = [(lambda t, k=k, o=o, Pb=Pb, ws=ws: t.matmul(
                        Pb[:, :], wsl[:, ws, k, o*128:(o+1)*128], ontb[:, k, :],
                        start=(k == 0), stop=(k == NCH-1))) for k in range(NCH)]
                    p.mm_group(fns, PB[bank], reads=[bwsl[ws], bont])
                    p.op("scalar", lambda s, og=og, o=o, bank=bank, Pb=Pb: s.activation(
                        ffno[:, og*4+o, :], Pb[:, :], AF.Copy),
                         reads=[PB[bank]], writes=[bffno])
            ln_stats(ffno, bffno)
            for c in range(NCH):
                p.dma("sync", out=xf32[:],
                      in_=xT.rearrange("(c p) t -> p c t", p=128)[:, c, tok0:tok0+ST],
                      writes=[bxf32])
                p.op("vector", lambda v, c=c: v.tensor_tensor(sqf[:], ffno[:, c, :], rbcf[:], ALU.mult),
                     reads=[bffno, bbc], writes=[bsqf])
                p.op("vector", lambda v, c=c: v.tensor_tensor(sqf[:], sqf[:], nmrbcf[:], ALU.add),
                     reads=[bsqf, bbc], writes=[bsqf])
                p.op("vector", lambda v, c=c: v.tensor_tensor(xf32[:], xf32[:], sqf[:], ALU.add),
                     reads=[bxf32, bsqf], writes=[bxf32])
                p.dma("gpsimd", out=xO.rearrange("(c p) t -> p c t", p=128)[:, c, tok0:tok0+ST],
                      in_=xf32[:], reads=[bxf32], writes=[bO])
        p.finish([bO])
        p.emit(block)
    es.close()
    return nc, p

D = 2048
I_MOE1 = 1024
I_SH1 = 4096
E = 8
NTOK = 2048
ST = 512
NST = NTOK // ST
NCH = D // 128
ICH1 = I_MOE1 // 128


def build_l1ffn(x_dtype=F32, out_dtype=F32):
    nc = bass.Bass("TRN2", target_bir_lowering=False)
    xT = nc.dram_tensor("xT", [D, NTOK], x_dtype, kind="ExternalInput").ap()
    W = {}
    def din(name, shape):
        W[name] = nc.dram_tensor(name, shape, F32, kind="ExternalInput").ap()
    din("w_gate", [D, E])
    din("w_gate_up", [E, D, 2 * I_MOE1])
    din("w_down", [E, I_MOE1, D])
    din("w_sh_gate_up", [D, 2 * I_SH1])
    din("w_sh_down", [I_SH1, D])
    din("ident", [128, 128])
    xO = nc.dram_tensor("xO", [D, NTOK], out_dtype, kind="ExternalOutput").ap()

    from contextlib import ExitStack
    es = ExitStack()
    T = lambda *a: es.enter_context(nc.sbuf_tensor(*a))
    xst = T("xst", [128, NCH, ST], BF16)
    xln = T("xln", [128, NCH, ST], BF16)
    gt = T("gtile", [128, ICH1, ST], BF16)
    ut = T("utile", [128, ICH1, ST], BF16)
    att = T("att", [128, ICH1, ST], BF16)
    osb = T("osb", [128, NCH, ST], F32)
    ffno = T("ffno", [128, NCH, ST], BF16)
    sq = T("sq", [128, ST], BF16)
    sqf = T("sqf", [128, ST], F32)
    wsl = T("wsl", [128, 2, 16, 512], BF16)
    wdn = T("wdn", [128, 2, 8, 1024], BF16)
    rwt = T("rwt", [128, NCH, 8], BF16)
    ones128 = T("ones128", [128, 1], BF16)
    onesr = T("onesr", [1, 128], F32)
    identt = T("identt", [128, 128], F32)
    epst = T("epst", [1, 1], F32)
    srow = T("srow", [1, ST], F32)
    qrow = T("qrow", [1, ST], F32)
    vrow = T("vrow", [1, ST], F32)
    rrow = T("rrow", [1, ST], F32)
    nmr = T("nmr", [1, ST], F32)
    rbcf = T("rbcf", [128, ST], F32)
    nmrbcf = T("nmrbcf", [128, ST], F32)
    m1 = T("m1", [128, 1], F32)
    pexp = T("pexp", [128, 8], F32)
    ssum = T("ssum", [128, 1], F32)
    rsum = T("rsum", [128, 1], F32)
    prob = T("prob", [128, 8], F32)
    v1t = T("v1", [128, 1], F32)
    v2t = T("v2", [128, 1], F32)
    mask1 = T("mask1", [128, 8], F32)
    mask2 = T("mask2", [128, 8], F32)
    ptmp = T("ptmp", [128, 8], F32)
    p2t = T("p2", [128, 8], F32)
    dent = T("den", [128, 1], F32)
    rdent = T("rden", [128, 1], F32)
    g12 = T("g12", [128, 2], F32)
    lam = T("lam", [128, 8], F32)
    lamT = T("lamT", [8, NTOK // 128, 128], F32)
    lrow = T("lrow", [1, ST], F32)
    lbc = T("lbc", [128, ST], BF16)
    xf32 = T("xf32", [128, 512], F32)
    P0 = es.enter_context(nc.psum_tensor("P0", [128, 512], F32))
    P1 = es.enter_context(nc.psum_tensor("P1", [128, 512], F32))
    P2 = es.enter_context(nc.psum_tensor("P2", [128, 512], F32))
    P3 = es.enter_context(nc.psum_tensor("P3", [128, 512], F32))
    P4 = es.enter_context(nc.psum_tensor("P4", [128, 512], F32))
    P5 = es.enter_context(nc.psum_tensor("P5", [128, 512], F32))
    with nc.Block() as block:
        p = Prog(nc)
        p.setup(dma_slots={"sync": 3, "gpsimd": 3})
        wt1 = [0]; wt2 = [0]
        PS = [P0, P1, P2, P3, P4, P5]
        PB = [Buf(f"P{i}") for i in range(6)]
        bxst, bxln, bgt, but, batt, bosb, bffno = (Buf(n) for n in
            ("xst", "xln", "gt", "ut", "att", "osb", "ffno"))
        bsq, bsqf, brwt = Buf("sq"), Buf("sqf"), Buf("rwt")
        bwsl = [Buf("wsl0"), Buf("wsl1")]
        bwdn = [Buf("wdn0"), Buf("wdn1")]
        bconst, brows, bbc, brt = Buf("const"), Buf("rows"), Buf("bc"), Buf("rt")
        blam, blamT, blrow, blbc = Buf("lam"), Buf("lamT"), Buf("lrow"), Buf("lbc")
        bxO = Buf("xO")
        bxf32 = Buf("xf32")

        p.op("vector", lambda v: v.memset(ones128[:], 1.0), writes=[bconst])
        p.op("vector", lambda v: v.memset(onesr[:], 1.0), writes=[bconst])
        p.op("vector", lambda v: v.memset(epst[:], 1e-5), writes=[bconst])
        p.dma("sync", out=identt[:], in_=W["ident"][:, :], writes=[bconst])
        p.dma("gpsimd", out=rwt[:], in_=W["w_gate"].rearrange("(c p) o -> p c o", p=128), writes=[brwt])

        def ln_stats(src, srcbuf):
            fns = [(lambda t, c=c: t.matmul(P0[0:1, :], ones128[:], src[:, c, :],
                                            start=(c == 0), stop=(c == NCH-1))) for c in range(NCH)]
            p.mm_group(fns, PB[0], reads=[srcbuf, bconst])
            p.op("scalar", lambda s: s.activation(srow[:], P0[0:1, :], AF.Copy, scale=1.0/D),
                 reads=[PB[0]], writes=[brows])
            for c in range(NCH):
                p.op("scalar", lambda s, c=c: s.activation(sq[:], src[:, c, :], AF.Square),
                     reads=[srcbuf], writes=[bsq])
                p.mm_group([lambda t, c=c: t.matmul(P1[0:1, :], ones128[:], sq[:],
                                                    start=(c == 0), stop=(c == NCH-1))],
                           PB[1], reads=[bsq, bconst])
            p.op("scalar", lambda s: s.activation(qrow[:], P1[0:1, :], AF.Copy, scale=1.0/D),
                 reads=[PB[1]], writes=[brows])
            p.op("vector", lambda v: v.tensor_tensor(vrow[:], srow[:], srow[:], ALU.mult),
                 reads=[brows], writes=[brows])
            p.op("vector", lambda v: v.tensor_tensor(vrow[:], qrow[:], vrow[:], ALU.subtract),
                 reads=[brows], writes=[brows])
            p.op("scalar", lambda s: s.activation(vrow[:], vrow[:], AF.Sqrt, bias=epst[:]),
                 reads=[brows, bconst], writes=[brows])
            p.op("vector", lambda v: v.reciprocal(rrow[:], vrow[:]), reads=[brows], writes=[brows])
            p.op("vector", lambda v: v.tensor_tensor(nmr[:], srow[:], rrow[:], ALU.mult),
                 reads=[brows], writes=[brows])
            p.op("vector", lambda v: v.tensor_scalar(nmr[:], nmr[:], -1.0, None, ALU.mult),
                 reads=[brows], writes=[brows])
            p.mm_group([lambda t: t.matmul(P2[:, :], onesr[:], rrow[:], start=True, stop=True)],
                       PB[2], reads=[brows, bconst])
            p.op("scalar", lambda s: s.activation(rbcf[:], P2[:, :], AF.Copy), reads=[PB[2]], writes=[bbc])
            p.mm_group([lambda t: t.matmul(P2[:, :], onesr[:], nmr[:], start=True, stop=True)],
                       PB[2], reads=[brows, bconst])
            p.op("scalar", lambda s: s.activation(nmrbcf[:], P2[:, :], AF.Copy), reads=[PB[2]], writes=[bbc])

        wgu = W["w_gate_up"].rearrange("e (c p) o -> e p c o", p=128)
        wdna = W["w_down"].rearrange("e (c p) o -> e p c o", p=128)
        wshgu = W["w_sh_gate_up"].rearrange("(c p) o -> p c o", p=128)
        wshd = W["w_sh_down"].rearrange("(c p) o -> p c o", p=128)

        for st in range(NST):
            tok0 = st * ST
            q = "gpsimd" if x_dtype == F32 else "sync"
            p.dma(q, out=xst[:], in_=xT.rearrange("(c p) t -> p c t", p=128)[:, :, tok0:tok0+ST],
                  writes=[bxst])
            ln_stats(xst, bxst)
            for c in range(NCH):
                p.op("vector", lambda v, c=c: v.tensor_tensor(sqf[:], xst[:, c, :], rbcf[:], ALU.mult),
                     reads=[bxst, bbc], writes=[bsqf])
                p.op("vector", lambda v, c=c: v.tensor_tensor(xln[:, c, :], sqf[:], nmrbcf[:], ALU.add),
                     reads=[bsqf, bbc], writes=[bxln])
            for tt in range(ST // 128):
                gi = (tok0 // 128) + tt
                fns = [(lambda t, k=k, tt=tt: t.matmul(P4[0:128, 0:8], xln[:, k, tt*128:(tt+1)*128],
                                                       rwt[:, k, :], start=(k == 0), stop=(k == NCH-1)))
                       for k in range(NCH)]
                p.mm_group(fns, PB[4], reads=[bxln, brwt])
                p.op("vector", lambda v: v.tensor_reduce(m1[:], P4[0:128, 0:8], axis=AX.X, op=ALU.max,
                                                         negate=True),
                     reads=[PB[4]], writes=[brt])
                p.op("scalar", lambda s: s.activation(pexp[:], P4[0:128, 0:8], AF.Exp, bias=m1[:],
                                                      accum_out=ssum[:]),
                     reads=[PB[4], brt], writes=[brt])
                p.op("vector", lambda v: v.reciprocal(rsum[:], ssum[:]), reads=[brt], writes=[brt])
                p.op("vector", lambda v: v.tensor_scalar(prob[:], pexp[:], rsum[:], None, ALU.mult),
                     reads=[brt], writes=[brt])
                p.op("vector", lambda v: v.tensor_reduce(v1t[:], prob[:], axis=AX.X, op=ALU.max),
                     reads=[brt], writes=[brt])
                p.op("vector", lambda v: v.tensor_scalar(mask1[:], prob[:], v1t[:], None, ALU.is_ge),
                     reads=[brt], writes=[brt])
                p.op("vector", lambda v: v.tensor_tensor(ptmp[:], prob[:], mask1[:], ALU.mult),
                     reads=[brt], writes=[brt])
                p.op("vector", lambda v: v.tensor_tensor(p2t[:], prob[:], ptmp[:], ALU.subtract),
                     reads=[brt], writes=[brt])
                p.op("vector", lambda v: v.tensor_reduce(v2t[:], p2t[:], axis=AX.X, op=ALU.max),
                     reads=[brt], writes=[brt])
                p.op("vector", lambda v: v.tensor_scalar(mask2[:], p2t[:], v2t[:], None, ALU.is_ge),
                     reads=[brt], writes=[brt])
                p.op("vector", lambda v: v.tensor_tensor(dent[:], v1t[:], v2t[:], ALU.add),
                     reads=[brt], writes=[brt])
                p.op("vector", lambda v: v.reciprocal(rdent[:], dent[:]), reads=[brt], writes=[brt])
                p.op("vector", lambda v: v.tensor_tensor(g12[:, 0:1], v1t[:], rdent[:], ALU.mult),
                     reads=[brt], writes=[brt])
                p.op("vector", lambda v: v.tensor_tensor(g12[:, 1:2], v2t[:], rdent[:], ALU.mult),
                     reads=[brt], writes=[brt])
                p.op("vector", lambda v: v.tensor_scalar(lam[:], mask1[:], g12[:, 0:1], None, ALU.mult),
                     reads=[brt], writes=[blam])
                p.op("vector", lambda v: v.scalar_tensor_tensor(lam[:], mask2[:], g12[:, 1:2], lam[:],
                                                                ALU.mult, ALU.add),
                     reads=[brt, blam], writes=[blam])
                p.op("tensor", lambda t: t.transpose(P5[0:8, 0:128], lam[:, 0:8], identt[:]),
                     reads=[blam, bconst], writes=[PB[5]])
                p.op("scalar", lambda s, gi=gi: s.activation(lamT[:, gi, :], P5[0:8, 0:128], AF.Copy),
                     reads=[PB[5]], writes=[blamT])

            for i in range(4):
                for half, (tile, tbuf, func) in enumerate((
                        (gt, bgt, AF.Silu), (ut, but, AF.Copy))):
                    base = i * 1024 + half * I_SH1
                    for og in range(2):
                        ws = wt1[0]; wt1[0] ^= 1
                        p.dma("gpsimd", out=wsl[:, ws, :, :], in_=wshgu[:, :, base+og*512:base+(og+1)*512],
                              writes=[bwsl[ws]])
                        for o in range(4):
                            bank = 3 if o % 2 == 0 else 4
                            fns = [(lambda t, k=k, o=o, bank=bank, ws=ws: t.matmul(
                                PS[bank][:, :], wsl[:, ws, k, o*128:(o+1)*128], xln[:, k, :],
                                start=(k == 0), stop=(k == NCH-1))) for k in range(NCH)]
                            p.mm_group(fns, PB[bank], reads=[bwsl[ws], bxln])
                            p.op("scalar", lambda s, o=o, og=og, bank=bank, tile=tile, func=func, tbuf=tbuf:
                                 s.activation(tile[:, og*4+o, :], PS[bank][:, :], func),
                                 reads=[PB[bank]], writes=[tbuf])
                for c in range(ICH1):
                    p.op("vector", lambda v, c=c: v.tensor_tensor(att[:, c, :], gt[:, c, :], ut[:, c, :],
                                                                  ALU.mult),
                         reads=[bgt, but], writes=[batt])
                for dg in range(2):
                    ws = wt2[0]; wt2[0] ^= 1
                    p.dma("gpsimd", out=wdn[:, ws, :, :], in_=wshd[:, i*8:(i+1)*8, dg*1024:(dg+1)*1024],
                          writes=[bwdn[ws]])
                    for dc in range(8):
                        bank = 3 if dc % 2 == 0 else 4
                        dchunk = dg * 8 + dc
                        fns = [(lambda t, k=k, dc=dc, bank=bank, ws=ws: t.matmul(
                            PS[bank][:, :], wdn[:, ws, k, dc*128:(dc+1)*128], att[:, k, :],
                            start=(k == 0), stop=(k == ICH1-1))) for k in range(ICH1)]
                        p.mm_group(fns, PB[bank], reads=[bwdn[ws], batt])
                        if st >= 0 and i == 0:
                            p.op("scalar", lambda s, dchunk=dchunk, bank=bank: s.activation(
                                osb[:, dchunk, :], PS[bank][:, :], AF.Copy),
                                 reads=[PB[bank]], writes=[bosb])
                        else:
                            p.op("vector", lambda v, dchunk=dchunk, bank=bank: v.tensor_tensor(
                                osb[:, dchunk, :], osb[:, dchunk, :], PS[bank][:, :], ALU.add),
                                 reads=[PB[bank], bosb], writes=[bosb])

            for e in range(E):
                p.dma("sync", out=lrow[:], in_=lamT[e:e+1, st*4:(st+1)*4, :].rearrange("p a b -> p (a b)"),
                      reads=[blamT], writes=[blrow])
                p.mm_group([lambda t: t.matmul(P5[:, :], onesr[:], lrow[:], start=True, stop=True)],
                           PB[5], reads=[blrow, bconst])
                p.op("scalar", lambda s: s.activation(lbc[:], P5[:, :], AF.Copy), reads=[PB[5]], writes=[blbc])
                for half, (tile, tbuf, func) in enumerate((
                        (gt, bgt, AF.Silu), (ut, but, AF.Copy))):
                    base = half * I_MOE1
                    for og in range(2):
                        ws = wt1[0]; wt1[0] ^= 1
                        p.dma("gpsimd", out=wsl[:, ws, :, :],
                              in_=wgu[e, :, :, base+og*512:base+(og+1)*512], writes=[bwsl[ws]])
                        for o in range(4):
                            bank = 3 if o % 2 == 0 else 4
                            fns = [(lambda t, k=k, o=o, bank=bank, ws=ws: t.matmul(
                                PS[bank][:, :], wsl[:, ws, k, o*128:(o+1)*128], xln[:, k, :],
                                start=(k == 0), stop=(k == NCH-1))) for k in range(NCH)]
                            p.mm_group(fns, PB[bank], reads=[bwsl[ws], bxln])
                            p.op("scalar", lambda s, o=o, og=og, bank=bank, tile=tile, func=func, tbuf=tbuf:
                                 s.activation(tile[:, og*4+o, :], PS[bank][:, :], func),
                                 reads=[PB[bank]], writes=[tbuf])
                for c in range(ICH1):
                    p.op("vector", lambda v, c=c: v.tensor_tensor(att[:, c, :], gt[:, c, :], ut[:, c, :],
                                                                  ALU.mult),
                         reads=[bgt, but], writes=[batt])
                    p.op("vector", lambda v, c=c: v.tensor_tensor(att[:, c, :], att[:, c, :], lbc[:],
                                                                  ALU.mult),
                         reads=[batt, blbc], writes=[batt])
                for dg in range(2):
                    ws = wt2[0]; wt2[0] ^= 1
                    p.dma("gpsimd", out=wdn[:, ws, :, :], in_=wdna[e, :, :, dg*1024:(dg+1)*1024],
                          writes=[bwdn[ws]])
                    for dc in range(8):
                        bank = 3 if dc % 2 == 0 else 4
                        dchunk = dg * 8 + dc
                        fns = [(lambda t, k=k, dc=dc, bank=bank, ws=ws: t.matmul(
                            PS[bank][:, :], wdn[:, ws, k, dc*128:(dc+1)*128], att[:, k, :],
                            start=(k == 0), stop=(k == ICH1-1))) for k in range(ICH1)]
                        p.mm_group(fns, PB[bank], reads=[bwdn[ws], batt])
                        p.op("vector", lambda v, dchunk=dchunk, bank=bank: v.tensor_tensor(
                            osb[:, dchunk, :], osb[:, dchunk, :], PS[bank][:, :], ALU.add),
                             reads=[PB[bank], bosb], writes=[bosb])
            for c in range(NCH):
                p.op("vector", lambda v, c=c: v.tensor_copy(ffno[:, c, :], osb[:, c, :]),
                     reads=[bosb], writes=[bffno])
            ln_stats(ffno, bffno)
            for c in range(NCH):
                p.op("vector", lambda v, c=c: v.tensor_tensor(sqf[:], ffno[:, c, :], rbcf[:], ALU.mult),
                     reads=[bffno, bbc], writes=[bsqf])
                p.op("vector", lambda v, c=c: v.tensor_tensor(sqf[:], sqf[:], nmrbcf[:], ALU.add),
                     reads=[bsqf, bbc], writes=[bsqf])
                p.dma("sync", out=xf32[:],
                      in_=xT.rearrange("(c p) t -> p c t", p=128)[:, c, tok0:tok0+ST],
                      writes=[bxf32])
                p.op("vector", lambda v, c=c: v.tensor_tensor(osb[:, c, :], xf32[:], sqf[:], ALU.add),
                     reads=[bxf32, bsqf], writes=[bosb])
            p.dma("gpsimd", out=xO.rearrange("(c p) t -> p c t", p=128)[:, :, tok0:tok0+ST],
                  in_=osb[:], reads=[bosb], writes=[bxO])
        p.finish([bxO])
        p.emit(block)
    es.close()
    return nc, p





_BF = ml_dtypes.bfloat16

def _asbf(a):
    a = np.asarray(a)
    if a.dtype == _BF:
        return a
    if a.dtype.itemsize == 2:
        return a.view(_BF)
    return a.astype(_BF)


def _prep_l0p2_consts():
    tm1 = np.triu(np.ones((64, 64), np.float32), 1)
    tm0 = np.triu(np.ones((64, 64), np.float32), 0)
    bd = np.zeros((64, 64), np.float32)
    bd[0:32, 0:32] = 1; bd[32:64, 32:64] = 1
    return np.ascontiguousarray(np.concatenate([tm1, tm0, bd], 1).astype(_BF))


_CACHE = {}

def _get(name, fn):
    if name not in _CACHE:
        _CACHE[name] = fn()
    return _CACHE[name]


def kernel(**inp):
    cores = list(range(8))
    ident = np.eye(128, dtype=np.float32)
    x0 = np.asarray(inp["L0_attn_hidden_states"], np.float32).reshape(16384, 2048)
    xT = np.ascontiguousarray(x0.T)

    W = {k[8:]: np.asarray(v) for k, v in inp.items() if k.startswith("L0_attn_")}
    wqkv = np.ascontiguousarray(np.concatenate([W["Wq"], W["Wk"], W["Wv"]], 1))
    nc1, _ = _get("l0p1", lambda: build_l0p1(x_dtype=F32))
    in1 = [{"xT": np.ascontiguousarray(xT[:, c*2048:(c+1)*2048]), "wqkv": wqkv,
            "Wf1": W["Wf1"], "Wb": W["Wb"]} for c in cores]
    r = run_bass_kernel_spmd(nc1, in1, core_ids=cores)
    qkvT = np.concatenate([_asbf(r.results[c]["qkvO"]) for c in cores], 1)
    f1T = np.concatenate([r.results[c]["f1O"] for c in cores], 1)
    betaT = np.concatenate([r.results[c]["betaO"] for c in cores], 1)

    nc2, _ = _get("l0p2", lambda: build_l0p2())
    TMASK = _prep_l0p2_consts()
    in2 = []
    for c in cores:
        b, g = c // 2, c % 2
        tok = slice(b*4096, (b+1)*4096)
        hs = slice(g*8*128, (g+1)*8*128)
        hh = slice(g*8, (g+1)*8)
        in2.append({
            "QR": np.ascontiguousarray(qkvT[0:2048][hs][:, tok]),
            "KR": np.ascontiguousarray(qkvT[2048:4096][hs][:, tok]),
            "VR": np.ascontiguousarray(qkvT[4096:6144][hs][:, tok]),
            "F1": np.ascontiguousarray(f1T[:, tok]),
            "BLOG": np.ascontiguousarray(betaT[hh][:, tok]),
            "CQ": np.ascontiguousarray(W["conv_q"].reshape(16, 128, 4)[hh].transpose(1, 0, 2)),
            "CK": np.ascontiguousarray(W["conv_k"].reshape(16, 128, 4)[hh].transpose(1, 0, 2)),
            "CV": np.ascontiguousarray(W["conv_v"].reshape(16, 128, 4)[hh].transpose(1, 0, 2)),
            "WF2": np.ascontiguousarray(W["Wf2"].reshape(128, 16, 128)[:, hh]),
            "DTB": np.ascontiguousarray(W["dt_bias"].reshape(16, 128)[hh].T),
            "ALOG": np.ascontiguousarray(W["A_log"][None, hh].astype(np.float32)),
            "ident": ident, "TMASK": TMASK,
        })
    r = run_bass_kernel_spmd(nc2, in2, core_ids=cores)
    onsF = np.zeros((2048, 16384), np.float32)
    for c in cores:
        b, g = c // 2, c % 2
        onsF[g*1024:(g+1)*1024, b*4096:(b+1)*4096] = r.results[c]["ON"]

    nc3, _ = _get("l0p3", lambda: build_l0p3(x_dtype=F32))
    in3 = [{"xT": np.ascontiguousarray(xT[:, c*2048:(c+1)*2048]),
            "onF": np.ascontiguousarray(onsF[:, c*2048:(c+1)*2048]),
            "Wg": W["Wg"], "Wo": W["Wo"],
            "bg": np.ascontiguousarray(W["bg"].reshape(-1, 128).T),
            "onw": np.ascontiguousarray(W["o_norm_w"].reshape(128, 1).astype(np.float32))}
           for c in cores]
    r = run_bass_kernel_spmd(nc3, in3, core_ids=cores)
    x1 = [r.results[c]["xO"] for c in cores]

    Wf = {k[7:]: np.asarray(v) for k, v in inp.items() if k.startswith("L0_ffn_")}
    cost = np.array([2 * 1024 * h for h in HID0], np.float32)
    clc = (COST_LAMBDA0 * cost)[None, :]
    nc4, _ = _get("l0ffn", lambda: build_l0ffn(x_dtype=F32))
    in4 = []
    for c in cores:
        m = {"xT": x1[c], "ident": ident, "clc": clc}
        for k, v in Wf.items():
            if k in ("down_b", "up_b", "shared_b1", "shared_b2", "core_b") or \
               (k.endswith("_b1") and k.startswith("e")):
                m[k] = np.ascontiguousarray(v.reshape(-1, 128).T)
            else:
                m[k] = v
        in4.append(m)
    r = run_bass_kernel_spmd(nc4, in4, core_ids=cores)
    x2 = [r.results[c]["xO"] for c in cores]

    Wa = {k[8:]: np.asarray(v) for k, v in inp.items() if k.startswith("L1_attn_")}
    nc5, _ = _get("l1p1", lambda: build_l1p1(x_dtype=F32))
    in5 = [{"xT": x2[c], "w_qkv": Wa["w_qkv"]} for c in cores]
    r = run_bass_kernel_spmd(nc5, in5, core_ids=cores)
    qkv1 = np.concatenate([r.results[c]["qkvO"] for c in cores], 1)

    nc6, _ = _get("l1p2", lambda: build_l1p2())
    LVm = np.stack([Wa["lambda_q1"], Wa["lambda_k1"], Wa["lambda_q2"], Wa["lambda_k2"]],
                   1).astype(np.float32)
    in6 = []
    for c in cores:
        b, g = c // 2, c % 2
        tok = slice(b*4096, (b+1)*4096)
        def rows(vh):
            return (qkv1[vh*64:(vh+1)*64, tok], qkv1[(16+vh)*64:(17+vh)*64, tok],
                    qkv1[2048+vh*64:2048+(vh+1)*64, tok],
                    qkv1[2048+(16+vh)*64:2048+(17+vh)*64, tok],
                    qkv1[4096+vh*128:4096+(vh+1)*128, tok])
        rs = [rows(g*8+u) for u in range(8)]
        in6.append({
            "Q1": np.ascontiguousarray(np.concatenate([a[0] for a in rs], 0)),
            "Q2": np.ascontiguousarray(np.concatenate([a[1] for a in rs], 0)),
            "K1": np.ascontiguousarray(np.concatenate([a[2] for a in rs], 0)),
            "K2": np.ascontiguousarray(np.concatenate([a[3] for a in rs], 0)),
            "V": np.ascontiguousarray(np.concatenate([a[4] for a in rs], 0)),
            "LV": LVm, "ident": ident,
        })
    r = run_bass_kernel_spmd(nc6, in6, core_ids=cores)
    ons1F = np.zeros((2048, 16384), np.float32)
    for c in cores:
        b, g = c // 2, c % 2
        ons1F[g*1024:(g+1)*1024, b*4096:(b+1)*4096] = r.results[c]["ON"]

    nc7, _ = _get("l1p3", lambda: build_l1p3())
    in7 = [{"xT": x2[c], "onF": np.ascontiguousarray(ons1F[:, c*2048:(c+1)*2048]),
            "w_proj": Wa["w_proj"],
            "rms_scale": np.ascontiguousarray(Wa["rms_scale"].reshape(128, 1).astype(np.float32))}
           for c in cores]
    r = run_bass_kernel_spmd(nc7, in7, core_ids=cores)
    x3 = [r.results[c]["xO"] for c in cores]

    Wm = {k[7:]: np.asarray(v) for k, v in inp.items() if k.startswith("L1_ffn_")}
    nc8, _ = _get("l1ffn", lambda: build_l1ffn(x_dtype=F32, out_dtype=F32))
    in8 = [{"xT": x3[c], "ident": ident, "w_gate": Wm["w_gate"],
            "w_gate_up": Wm["w_gate_up"], "w_down": Wm["w_down"],
            "w_sh_gate_up": Wm["w_sh_gate_up"], "w_sh_down": Wm["w_sh_down"]}
           for c in cores]
    r = run_bass_kernel_spmd(nc8, in8, core_ids=cores)
    out = np.concatenate([r.results[c]["xO"].astype(np.float32).T for c in cores], 0)
    return out.reshape(4, 4096, 2048)
